# version 02
# baseline (speedup 1.0000x reference)
import numpy as np
import ml_dtypes
import concourse.bass as bass
import concourse.mybir as mybir
from concourse.bass_utils import run_bass_kernel_spmd
from contextlib import ExitStack

F32 = mybir.dt.float32
BF16 = mybir.dt.bfloat16
AF = mybir.ActivationFunctionType
ALU = mybir.AluOpType
AX = mybir.AxisListType

ENGINES = ("tensor", "vector", "scalar", "sync", "gpsimd")

N = 4096
D = 1024
TT = N // 128
DG = D // 128
HEADS = 16
HD = 64
DI = 2048
DIG = DI // 128
DS = 16
EPS = 1e-5


class Em:
    def __init__(self):
        self.items = []
        self.val = 0

    def op(self, eng, fn, inc=1):
        self.items.append((eng, self.val, inc, fn))
        self.val += inc

    def group(self, eng, fns, inc=1):
        w = self.val
        for i, fn in enumerate(fns):
            self.items.append((eng, w if i == 0 else None, inc, fn))
            self.val += inc

    def dma(self, dst, src, eng="sync"):
        self.op(eng, lambda e, d=dst, s=src: e.dma_start(d, s), inc=16)

    def dma_group(self, pairs, eng="sync"):
        self.group(eng, [lambda e, d=d_, s=s_: e.dma_start(d, s) for (d_, s_) in pairs], inc=16)

    def build(self, nc, tok):
        final = self.val
        with nc.Block() as block:
            for eng_name in ENGINES:
                ops = [it for it in self.items if it[0] == eng_name]
                if not ops and eng_name != "sync":
                    continue

                def section(engine, ops=ops, eng_name=eng_name):
                    for (_, wait, inc, fn) in ops:
                        if wait is not None and wait > 0:
                            engine.wait_ge(tok, wait)
                        inst = fn(engine)
                        if inc:
                            inst.then_inc(tok, inc)
                    if eng_name == "sync":
                        engine.wait_ge(tok, final)

                getattr(block, eng_name)(section)


def emit_ln(em, s, out, work, nsum, sqt, ceps):
    em.op("vector", lambda e: e.tensor_reduce(nsum[:, 0:1], s[:], axis=AX.X, op=ALU.add, negate=True))
    em.op("scalar", lambda e: e.activation(work[:, 0:1], nsum[:, 0:1], AF.Copy, scale=1.0 / D))
    em.op("vector", lambda e: e.tensor_scalar(s[:], s[:], work[:, 0:1], None, op0=ALU.add))
    em.op("scalar", lambda e: e.activation(sqt[:], s[:], AF.Square, accum_out=nsum[:, 0:1]))
    em.op("scalar", lambda e: e.activation(work[:, 0:1], nsum[:, 0:1], AF.Ln, scale=1.0 / D, bias=ceps[:, 0:1]))
    em.op("scalar", lambda e: e.activation(work[:, 0:1], work[:, 0:1], AF.Exp, scale=-0.5))
    em.op("vector", lambda e: e.tensor_scalar(out[:], s[:], work[:, 0:1], None, op0=ALU.mult))


def host_consts():
    inv = 1.0 / (10000.0 ** (np.arange(0, HD, 2, dtype=np.float64) / HD))
    freqs = np.arange(N, dtype=np.float64)[:, None] * inv[None, :]
    cosv = np.cos(freqs)
    sinv = np.sin(freqs)
    fidx = (np.arange(128) % 64) // 2
    cosk = cosv[:, fidx].T.astype(np.float32).copy()
    sink = sinv[:, fidx].T.astype(np.float32).copy()
    cosq = (0.125 * cosk).astype(np.float32)
    sinq = (0.125 * sink).astype(np.float32)
    rt = np.zeros((128, 128), np.float32)
    for b in (0, 64):
        for i in range(32):
            rt[b + 2 * i + 1, b + 2 * i] = -1.0
            rt[b + 2 * i, b + 2 * i + 1] = 1.0
    idn = np.eye(128, dtype=np.float32)
    return cosq, sinq, cosk, sink, rt, idn


def emit_l1(nc, em, x, wqkv, wproj, cosq, sinq, cosk, sink, rt, idn, x1):
    P = "a_"
    qf = nc.dram_tensor(P + "qf", [D, N], BF16, kind="Internal").ap()
    kf = nc.dram_tensor(P + "kf", [D, N], BF16, kind="Internal").ap()
    vf = nc.dram_tensor(P + "vf", [N, D], BF16, kind="Internal").ap()
    at = nc.dram_tensor(P + "at", [D, N], BF16, kind="Internal").ap()

    st = ExitStack()
    sb = lambda nm, sh, dt: st.enter_context(nc.sbuf_tensor(P + nm, sh, dt))
    ps = lambda nm, sh, dt: st.enter_context(nc.psum_tensor(P + nm, sh, dt))

    idn_f = sb("idn_f", [128, 128], F32)
    idn_b = sb("idn_b", [128, 128], BF16)
    rt_bf = sb("rt_bf", [128, 128], BF16)
    stage1 = sb("stage1", [128, 128], F32)
    xs = sb("xs", [128, D], F32)
    ones_b = sb("ones_b", [128, 1], BF16)
    ones_r = sb("ones_r", [1, 128], BF16)
    ps0 = ps("ps0", [128, 512], F32)
    ps1 = ps("ps1", [128, 512], F32)

    em.dma_group([(idn_f[:], idn[:, :]), (stage1[:], rt[:, :])])
    em.op("vector", lambda e: e.tensor_copy(idn_b[:], idn_f[:]))
    em.op("vector", lambda e: e.tensor_copy(rt_bf[:], stage1[:]))
    em.op("gpsimd", lambda e: e.memset(ones_b[:, 0:1], 1.0))
    em.op("gpsimd", lambda e: e.memset(ones_r[0:1, :], 1.0))

    stA = ExitStack()
    sbA = lambda nm, sh, dt: stA.enter_context(nc.sbuf_tensor(P + nm, sh, dt))
    xT = sbA("xT", [128, DG * N], BF16)
    cq = sbA("cq", [128, N], F32)
    sq_ = sbA("sq_", [128, N], F32)
    ck = sbA("ck", [128, N], F32)
    sk = sbA("sk", [128, N], F32)
    wst = sbA("wst", [128, DG * 128], F32)
    wbf = sbA("wbf", [128, DG * 128], BF16)
    qkb = sbA("qkb", [128, 512], BF16)
    t1 = sbA("t1", [128, 512], F32)
    t2 = sbA("t2", [128, 512], F32)
    obf = sbA("obf", [128, 512], BF16)
    vbt = sbA("vbt", [128, 128], BF16)

    em.dma_group([(cq[:], cosq[:, :]), (sq_[:], sinq[:, :]), (ck[:], cosk[:, :]), (sk[:], sink[:, :])])

    for tt in range(TT):
        em.dma(xs[:], x[tt * 128:(tt + 1) * 128, :])
        for g in range(DG):
            em.op("tensor", lambda e, g=g: e.transpose(ps0[:, 0:128], xs[:, g * 128:(g + 1) * 128], idn_f[:]))
            em.op("vector", lambda e, g=g, tt=tt: e.tensor_copy(xT[:, g * N + tt * 128: g * N + tt * 128 + 128], ps0[:, 0:128]))

    wqkv_r = wqkv.rearrange("(g p) m -> p g m", p=128)
    for m in range(16):
        em.dma(wst.rearrange("p (g m) -> p g m", g=DG)[:, :, :], wqkv_r[:, :, m * 128:(m + 1) * 128])
        em.op("vector", lambda e: e.tensor_copy(wbf[:], wst[:]))
        is_q = m < 8
        ctab = cq if is_q else ck
        stab = sq_ if is_q else sk
        dst = qf if is_q else kf
        row = (m if is_q else m - 8) * 128
        for n in range(8):
            em.group("tensor", [
                (lambda e, kg=kg, n=n: e.matmul(ps0[:, 0:512], wbf[:, kg * 128:(kg + 1) * 128],
                                                xT[:, kg * N + n * 512: kg * N + (n + 1) * 512],
                                                start=(kg == 0), stop=(kg == DG - 1)))
                for kg in range(DG)])
            em.op("vector", lambda e: e.tensor_copy(qkb[:], ps0[:, 0:512]))
            em.op("tensor", lambda e: e.matmul(ps1[:, 0:512], rt_bf[:], qkb[:], start=True, stop=True))
            em.op("vector", lambda e, n=n, ctab=ctab: e.tensor_tensor(t1[:], ps0[:, 0:512], ctab[:, n * 512:(n + 1) * 512], op=ALU.mult))
            em.op("vector", lambda e, n=n, stab=stab: e.tensor_tensor(t2[:], ps1[:, 0:512], stab[:, n * 512:(n + 1) * 512], op=ALU.mult))
            em.op("vector", lambda e: e.tensor_tensor(obf[:], t1[:], t2[:], op=ALU.add))
            em.dma(dst[row:row + 128, n * 512:(n + 1) * 512], obf[:])
    for m in range(8):
        em.dma(wst.rearrange("p (g m) -> p g m", g=DG)[:, :, :], wqkv_r[:, :, (16 + m) * 128:(17 + m) * 128])
        em.op("vector", lambda e: e.tensor_copy(wbf[:], wst[:]))
        for tt in range(TT):
            em.group("tensor", [
                (lambda e, kg=kg, tt=tt: e.matmul(ps0[:, 0:128],
                                                  xT[:, kg * N + tt * 128: kg * N + (tt + 1) * 128],
                                                  wbf[:, kg * 128:(kg + 1) * 128],
                                                  start=(kg == 0), stop=(kg == DG - 1)))
                for kg in range(DG)])
            em.op("vector", lambda e: e.tensor_copy(vbt[:], ps0[:, 0:128]))
            em.dma(vf[tt * 128:(tt + 1) * 128, m * 128:(m + 1) * 128], vbt[:])
    stA.close()

    stB = ExitStack()
    sbB = lambda nm, sh, dt: stB.enter_context(nc.sbuf_tensor(P + nm, sh, dt))
    qh = sbB("qh", [64, N], BF16)
    kh = sbB("kh", [64, N], BF16)
    vh = sbB("vh", [128, TT * 64], BF16)
    est = sbB("est", [128, TT * 512], BF16)
    rr = sbB("rr", [1, 512], F32)
    rrb = sbB("rrb", [1, 512], BF16)
    bcs = sbB("bcs", [64, 512], F32)
    ybf = sbB("ybf", [64, 512], BF16)
    psST = ps("psST", [128, 512], F32)
    psYT = ps("psYT", [128, 512], F32)
    psR1 = ps("psR1", [128, 512], F32)
    psBC = ps("psBC", [128, 512], F32)

    vf_r = vf.rearrange("(tt p) c -> p tt c", p=128)
    for h in range(HEADS):
        em.dma(qh[:, :], qf[h * 64:(h + 1) * 64, :])
        em.dma(kh[:, :], kf[h * 64:(h + 1) * 64, :])
        em.dma_group([
            (vh.rearrange("p (tt c) -> p tt c", c=64)[:, q * 8:(q + 1) * 8, :],
             vf_r[:, q * 8:(q + 1) * 8, h * 64:(h + 1) * 64])
            for q in range(4)])
        for tqn in range(8):
            for tkb in range(TT):
                em.op("tensor", lambda e, tkb=tkb, tqn=tqn: e.matmul(
                    psST[:, 0:512], kh[:, tkb * 128:(tkb + 1) * 128],
                    qh[:, tqn * 512:(tqn + 1) * 512], start=True, stop=True))
                em.op("scalar", lambda e, tkb=tkb: e.activation(
                    est[:, tkb * 512:(tkb + 1) * 512], psST[:, 0:512], AF.Exp))
            fns = []
            for tkb in range(TT):
                fns.append(lambda e, tkb=tkb: e.matmul(
                    psYT[0:64, 0:512], vh[:, tkb * 64:(tkb + 1) * 64],
                    est[:, tkb * 512:(tkb + 1) * 512],
                    start=(tkb == 0), stop=(tkb == TT - 1)))
                fns.append(lambda e, tkb=tkb: e.matmul(
                    psR1[0:1, 0:512], ones_b[:, 0:1],
                    est[:, tkb * 512:(tkb + 1) * 512],
                    start=(tkb == 0), stop=(tkb == TT - 1), skip_group_check=True))
            em.group("tensor", fns)
            em.op("vector", lambda e: e.reciprocal(rr[0:1, :], psR1[0:1, 0:512]))
            em.op("vector", lambda e: e.tensor_copy(rrb[0:1, :], rr[0:1, :]))
            em.op("tensor", lambda e: e.matmul(psBC[0:64, 0:512], ones_r[0:1, 0:64], rrb[0:1, :], start=True, stop=True))
            em.op("vector", lambda e: e.tensor_copy(bcs[:, :], psBC[0:64, 0:512]))
            em.op("vector", lambda e: e.tensor_tensor(ybf[:, :], psYT[0:64, 0:512], bcs[:, :], op=ALU.mult))
            em.dma(at[h * 64:(h + 1) * 64, tqn * 512:(tqn + 1) * 512], ybf[:, :])
    stB.close()

    stC = ExitStack()
    sbC = lambda nm, sh, dt: stC.enter_context(nc.sbuf_tensor(P + nm, sh, dt))
    aT = sbC("aT", [128, DG * N], BF16)
    wpst = sbC("wpst", [128, D], F32)
    wpbf = sbC("wpbf", [128, DG * D], BF16)
    xres = sbC("xres", [128, D], F32)
    s_ = sbC("s_", [128, D], F32)
    sqt = sbC("sqt", [128, D], F32)
    nsum = sbC("nsum", [128, 1], F32)
    ceps = sbC("ceps", [128, 1], F32)
    _ = em.op("vector", lambda e: e.memset(ceps[:, 0:1], EPS))
    wk1 = sbC("wk1", [128, 1], F32)
    o_ = sbC("o_", [128, D], F32)

    for g in range(DG):
        em.dma(aT[:, g * N:(g + 1) * N], at[g * 128:(g + 1) * 128, :])
    for kg in range(DG):
        em.dma(wpst[:], wproj[kg * 128:(kg + 1) * 128, :])
        em.op("vector", lambda e, kg=kg: e.tensor_copy(wpbf[:, kg * D:(kg + 1) * D], wpst[:]))
    for tt in range(TT):
        for nn in range(2):
            pt = ps0 if nn == 0 else ps1
            em.group("tensor", [
                (lambda e, kg=kg, tt=tt, nn=nn, pt=pt: e.matmul(
                    pt[:, 0:512], aT[:, kg * N + tt * 128: kg * N + (tt + 1) * 128],
                    wpbf[:, kg * D + nn * 512: kg * D + (nn + 1) * 512],
                    start=(kg == 0), stop=(kg == DG - 1)))
                for kg in range(DG)])
        em.dma(xres[:], x[tt * 128:(tt + 1) * 128, :])
        em.op("vector", lambda e: e.tensor_tensor(s_[:, 0:512], ps0[:, 0:512], xres[:, 0:512], op=ALU.add))
        em.op("vector", lambda e: e.tensor_tensor(s_[:, 512:1024], ps1[:, 0:512], xres[:, 512:1024], op=ALU.add))
        emit_ln(em, s_, o_, wk1, nsum, sqt, ceps)
        em.dma(x1[tt * 128:(tt + 1) * 128, :], o_[:])
    stC.close()
    st.close()


def emit_l2(nc, em, x1, rw, rb, lw, lb, w1, b1, w2, b2, idn, x2):
    P = "f_"
    st = ExitStack()
    sb = lambda nm, sh, dt: st.enter_context(nc.sbuf_tensor(P + nm, sh, dt))
    ps_ = lambda nm, sh, dt: st.enter_context(nc.psum_tensor(P + nm, sh, dt))

    idn_f = sb("idn_f", [128, 128], F32)
    ones_r = sb("ones_r", [1, 128], BF16)
    x1T = sb("x1T", [128, DG * N], BF16)
    xs = sb("xs", [128, D], F32)
    Wr = sb("Wr", [128, TT * 8], F32)
    lwT = sb("lwT", [128, DG * 4], BF16)
    rwT = sb("rwT", [128, DG * 8], BF16)
    blang = sb("blang", [128, 4], F32)
    brout = sb("brout", [128, 8], F32)
    rowf = sb("rowf", [1, 1024], F32)
    rowb = sb("rowb", [1, 1024], BF16)
    ll = sb("ll", [128, 8], F32)
    lp = sb("lp", [128, 4], F32)
    ew = sb("ew", [128, 4], F32)
    pp = sb("pp", [128, 8], F32)
    ge1 = sb("ge1", [128, 8], F32)
    m1 = sb("m1", [128, 1], F32)
    m2 = sb("m2", [128, 1], F32)
    wk1 = sb("wk1", [128, 1], F32)
    h_bf = sb("h_bf", [128, 32 * 512], BF16)
    b1sb = sb("b1sb", [128, 8 * 32], F32)
    b2bc = sb("b2bc", [128, D], F32)
    w1st = sb("w1st", [128, DG * 128], F32)
    w1bf = sb("w1bf", [128, DG * 128], BF16)
    w2st = sb("w2st", [128, D], F32)
    w2bf = sb("w2bf", [128, D], BF16)
    tz = sb("tz", [128, 512], F32)
    acc = sb("acc", [128, 4 * D], F32)
    xres = sb("xres", [128, D], F32)
    sqt = sb("sqt", [128, D], F32)
    nsum = sb("nsum", [128, 1], F32)
    ceps = sb("ceps", [128, 1], F32)
    _ = em.op("vector", lambda e: e.memset(ceps[:, 0:1], EPS))
    o_ = sb("o_", [128, D], F32)
    pz = [ps_("pz%d" % i, [128, 512], F32) for i in range(8)]

    em.dma(idn_f[:], idn[:, :])
    em.op("gpsimd", lambda e: e.memset(ones_r[0:1, :], 1.0))
    em.dma(xs[0:4, :], lw[:, :])
    for g in range(DG):
        em.op("tensor", lambda e, g=g: e.transpose(pz[0][:, 0:4], xs[0:4, g * 128:(g + 1) * 128], idn_f[0:4, 0:4]))
        em.op("vector", lambda e, g=g: e.tensor_copy(lwT[:, g * 4:(g + 1) * 4], pz[0][:, 0:4]))
    em.dma(xs[0:8, :], rw[:, :])
    for g in range(DG):
        em.op("tensor", lambda e, g=g: e.transpose(pz[0][:, 0:8], xs[0:8, g * 128:(g + 1) * 128], idn_f[0:8, 0:8]))
        em.op("vector", lambda e, g=g: e.tensor_copy(rwT[:, g * 8:(g + 1) * 8], pz[0][:, 0:8]))
    em.dma(rowf[0:1, 0:4], lb[:].rearrange("(o c) -> o c", o=1))
    em.op("vector", lambda e: e.tensor_copy(rowb[0:1, 0:4], rowf[0:1, 0:4]))
    em.op("tensor", lambda e: e.matmul(pz[0][:, 0:4], ones_r[0:1, 0:128], rowb[0:1, 0:4], start=True, stop=True))
    em.op("vector", lambda e: e.tensor_copy(blang[:], pz[0][:, 0:4]))
    em.dma(rowf[0:1, 0:8], rb[:].rearrange("(o c) -> o c", o=1))
    em.op("vector", lambda e: e.tensor_copy(rowb[0:1, 0:8], rowf[0:1, 0:8]))
    em.op("tensor", lambda e: e.matmul(pz[0][:, 0:8], ones_r[0:1, 0:128], rowb[0:1, 0:8], start=True, stop=True))
    em.op("vector", lambda e: e.tensor_copy(brout[:], pz[0][:, 0:8]))
    for e_ in range(8):
        em.dma_group([
            (b1sb[:, e_ * 32 + m: e_ * 32 + m + 1], b1[e_, m * 128:(m + 1) * 128].rearrange("(p o) -> p o", o=1))
            for m in range(32)])

    for tt in range(TT):
        em.dma(xs[:], x1[tt * 128:(tt + 1) * 128, :])
        for g in range(DG):
            em.op("tensor", lambda e, g=g: e.transpose(pz[1][:, 0:128], xs[:, g * 128:(g + 1) * 128], idn_f[:]))
            em.op("vector", lambda e, g=g, tt=tt: e.tensor_copy(x1T[:, g * N + tt * 128: g * N + tt * 128 + 128], pz[1][:, 0:128]))
        em.group("tensor", [
            (lambda e, g=g, tt=tt: e.matmul(pz[0][:, 0:4], x1T[:, g * N + tt * 128: g * N + (tt) * 128 + 128],
                                            lwT[:, g * 4:(g + 1) * 4], start=(g == 0), stop=(g == DG - 1)))
            for g in range(DG)])
        em.op("vector", lambda e: e.tensor_tensor(lp[:], pz[0][:, 0:4], blang[:], op=ALU.add))
        em.op("vector", lambda e: e.reduce_max(m1[:, 0:1], lp[:], axis=AX.X, negate=True))
        em.op("scalar", lambda e: e.activation(lp[:], lp[:], AF.Exp, bias=m1[:, 0:1], accum_out=nsum[:, 0:1]))
        em.op("vector", lambda e: e.reciprocal(wk1[:, 0:1], nsum[:, 0:1]))
        em.op("vector", lambda e: e.tensor_scalar(lp[:], lp[:], wk1[:, 0:1], None, op0=ALU.mult))
        em.op("vector", lambda e: e.tensor_tensor(ew[:, 0:3], lp[:, 0:3], lp[:, 1:4], op=ALU.add))
        em.op("vector", lambda e: e.tensor_copy(ew[:, 3:4], lp[:, 3:4]))
        em.group("tensor", [
            (lambda e, g=g, tt=tt: e.matmul(pz[0][:, 0:8], x1T[:, g * N + tt * 128: g * N + tt * 128 + 128],
                                            rwT[:, g * 8:(g + 1) * 8], start=(g == 0), stop=(g == DG - 1)))
            for g in range(DG)])
        em.op("vector", lambda e: e.tensor_tensor(ll[:], pz[0][:, 0:8], brout[:], op=ALU.add))
        em.op("vector", lambda e: e.scalar_tensor_tensor(ll[:, 0:4], ew[:], 0.1, ll[:, 0:4], op0=ALU.mult, op1=ALU.add))
        em.op("vector", lambda e: e.scalar_tensor_tensor(ll[:, 4:8], ew[:], 0.1, ll[:, 4:8], op0=ALU.mult, op1=ALU.add))
        em.op("vector", lambda e: e.reduce_max(m1[:, 0:1], ll[:], axis=AX.X, negate=True))
        em.op("scalar", lambda e: e.activation(pp[:], ll[:], AF.Exp, bias=m1[:, 0:1], accum_out=nsum[:, 0:1]))
        em.op("vector", lambda e: e.reciprocal(wk1[:, 0:1], nsum[:, 0:1]))
        em.op("vector", lambda e: e.tensor_scalar(pp[:], pp[:], wk1[:, 0:1], None, op0=ALU.mult))
        em.op("vector", lambda e: e.reduce_max(m1[:, 0:1], pp[:], axis=AX.X))
        em.op("vector", lambda e: e.tensor_scalar(ge1[:], pp[:], m1[:, 0:1], None, op0=ALU.is_ge))
        em.op("vector", lambda e: e.scalar_tensor_tensor(ge1[:], ge1[:], -2.0, pp[:], op0=ALU.mult, op1=ALU.add))
        em.op("vector", lambda e: e.reduce_max(m2[:, 0:1], ge1[:], axis=AX.X))
        em.op("vector", lambda e: e.tensor_tensor(wk1[:, 0:1], m1[:, 0:1], m2[:, 0:1], op=ALU.add))
        em.op("vector", lambda e: e.reciprocal(wk1[:, 0:1], wk1[:, 0:1]))
        em.op("vector", lambda e: e.tensor_scalar(ge1[:], pp[:], m2[:, 0:1], None, op0=ALU.is_ge))
        em.op("vector", lambda e: e.tensor_tensor(ge1[:], pp[:], ge1[:], op=ALU.mult))
        em.op("vector", lambda e, tt=tt: e.tensor_scalar(Wr[:, tt * 8:(tt + 1) * 8], ge1[:], wk1[:, 0:1], None, op0=ALU.mult))

    w1r = [w1[e_].rearrange("(g p) m -> p g m", p=128) for e_ in range(8)]
    for c in range(8):
        for e_ in range(8):
            em.dma(rowf[0:1, :], b2[e_:e_ + 1, :])
            em.op("vector", lambda e: e.tensor_copy(rowb[0:1, :], rowf[0:1, :]))
            em.op("tensor", lambda e: e.matmul(pz[0][:, 0:512], ones_r[0:1, 0:128], rowb[0:1, 0:512], start=True, stop=True))
            em.op("tensor", lambda e: e.matmul(pz[1][:, 0:512], ones_r[0:1, 0:128], rowb[0:1, 512:1024], start=True, stop=True))
            em.op("vector", lambda e: e.tensor_copy(b2bc[:, 0:512], pz[0][:, 0:512]))
            em.op("vector", lambda e: e.tensor_copy(b2bc[:, 512:1024], pz[1][:, 0:512]))
            for m in range(32):
                em.dma(w1st.rearrange("p (g m) -> p g m", g=DG)[:, :, :], w1r[e_][:, :, m * 128:(m + 1) * 128])
                em.op("vector", lambda e: e.tensor_copy(w1bf[:], w1st[:]))
                em.group("tensor", [
                    (lambda e, kg=kg, c=c: e.matmul(pz[7][:, 0:512], w1bf[:, kg * 128:(kg + 1) * 128],
                                                    x1T[:, kg * N + c * 512: kg * N + (c + 1) * 512],
                                                    start=(kg == 0), stop=(kg == DG - 1)))
                    for kg in range(DG)])
                em.op("scalar", lambda e, m=m, e_=e_: e.activation(
                    h_bf[:, m * 512:(m + 1) * 512], pz[7][:, 0:512], AF.Gelu,
                    bias=b1sb[:, e_ * 32 + m: e_ * 32 + m + 1]))
            for kg in range(32):
                em.dma(w2st[:], w2[e_, kg * 128:(kg + 1) * 128, :])
                em.op("vector", lambda e: e.tensor_copy(w2bf[:], w2st[:]))
                fns = []
                for ts in range(4):
                    for nn in range(2):
                        fns.append(lambda e, kg=kg, ts=ts, nn=nn: e.matmul(
                            pz[ts * 2 + nn][:, 0:512],
                            h_bf[:, kg * 512 + ts * 128: kg * 512 + (ts + 1) * 128],
                            w2bf[:, nn * 512:(nn + 1) * 512],
                            start=(kg == 0), stop=(kg == 31)))
                em.group("tensor", fns)
            for ts in range(4):
                for nn in range(2):
                    em.op("vector", lambda e, ts=ts, nn=nn: e.tensor_tensor(
                        tz[:], pz[ts * 2 + nn][:, 0:512], b2bc[:, nn * 512:(nn + 1) * 512], op=ALU.add))
                    if e_ == 0:
                        em.op("vector", lambda e, ts=ts, nn=nn, c=c: e.tensor_scalar(
                            acc[:, ts * D + nn * 512: ts * D + (nn + 1) * 512], tz[:],
                            Wr[:, (c * 4 + ts) * 8 + 0: (c * 4 + ts) * 8 + 1], None, op0=ALU.mult))
                    else:
                        em.op("vector", lambda e, ts=ts, nn=nn, c=c, e_=e_: e.scalar_tensor_tensor(
                            acc[:, ts * D + nn * 512: ts * D + (nn + 1) * 512], tz[:],
                            Wr[:, (c * 4 + ts) * 8 + e_: (c * 4 + ts) * 8 + e_ + 1],
                            acc[:, ts * D + nn * 512: ts * D + (nn + 1) * 512], op0=ALU.mult, op1=ALU.add))
        for ts in range(4):
            tt = c * 4 + ts
            em.dma(xres[:], x1[tt * 128:(tt + 1) * 128, :])
            em.op("vector", lambda e, ts=ts: e.tensor_tensor(xres[:], xres[:], acc[:, ts * D:(ts + 1) * D], op=ALU.add))
            emit_ln(em, xres, o_, wk1, nsum, sqt, ceps)
            em.dma(x2[tt * 128:(tt + 1) * 128, :], o_[:])
    st.close()


def emit_l3(nc, em, x2, nw, nb, ipw, ipb, cw, cb, xpw, xpb, dpw, dpb, alog, dpar, opw, opb, idn, x3):
    P = "m_"
    ipT = nc.dram_tensor(P + "ipT", [D, 2 * DI], BF16, kind="Internal").ap()
    dpT = nc.dram_tensor(P + "dpT", [DI, DI], BF16, kind="Internal").ap()
    opT = nc.dram_tensor(P + "opT", [DI, D], BF16, kind="Internal").ap()

    st = ExitStack()
    sb = lambda nm, sh, dt: st.enter_context(nc.sbuf_tensor(P + nm, sh, dt))
    ps_ = lambda nm, sh, dt: st.enter_context(nc.psum_tensor(P + nm, sh, dt))

    CH = 512
    NCH = N // CH

    idn_f = sb("idn_f", [128, 128], F32)
    ones_r = sb("ones_r", [1, 128], BF16)
    stg = sb("stg", [128, 1024], F32)
    tbf = sb("tbf", [128, 128], BF16)
    nw_bc = sb("nw_bc", [128, D], F32)
    nb_bc = sb("nb_bc", [128, D], F32)
    ob_bc = sb("ob_bc", [128, D], F32)
    rowf = sb("rowf", [1, 1024], F32)
    rowb = sb("rowb", [1, 1024], BF16)
    ipb_sb = sb("ipb_sb", [128, 32], F32)
    cw_sb = sb("cw_sb", [128, DIG * 4], F32)
    cb_sb = sb("cb_sb", [128, DIG], F32)
    xpb_sb = sb("xpb_sb", [32, 1], F32)
    dpb_sb = sb("dpb_sb", [128, DIG], F32)
    an_sb = sb("an_sb", [128, DIG * DS], F32)
    dp_sb = sb("dp_sb", [128, DIG], F32)
    xps_bf = sb("xps_bf", [128, DIG * 32], BF16)
    hstate = sb("hstate", [128, DIG * DS], F32)
    xnT = sb("xnT", [128, DG * CH], BF16)
    xc_bf = sb("xc_bf", [128, DIG * (CH + 3)], BF16)
    z_bf = sb("z_bf", [128, DIG * CH], BF16)
    u_bf = sb("u_bf", [128, DIG * CH], BF16)
    delta = sb("delta", [128, DIG * CH], BF16)
    du_bf = sb("du_bf", [128, DIG * CH], BF16)
    bcB = sb("bcB", [128, DS * CH], BF16)
    bcC = sb("bcC", [128, DS * CH], BF16)
    wstr = sb("wstr", [128, 16 * 128], BF16)
    sp0 = sb("sp0", [128, CH], F32)
    sp1 = sb("sp1", [128, CH], F32)
    sp2 = sb("sp2", [128, CH], F32)
    xdb = sb("xdb", [32, CH], F32)
    rws = sb("rws", [1, CH], F32)
    rwb = sb("rwb", [1, CH], BF16)
    dab = sb("dab", [128, CH], BF16)
    dbu = sb("dbu", [128, CH], BF16)
    hsc = sb("hsc", [128, CH], BF16)
    yg = sb("yg", [128, CH], F32)
    ybf = z_bf
    otst = delta
    ytmp = sb("ytmp", [128, CH], F32)
    zsil = sb("zsil", [128, CH], F32)
    s_tm = sb("s_tm", [128, D], F32)
    xres = sb("xres", [128, D], F32)
    sqt = sb("sqt", [128, D], F32)
    nsum = sb("nsum", [128, 1], F32)
    ceps = sb("ceps", [128, 1], F32)
    _ = em.op("vector", lambda e: e.memset(ceps[:, 0:1], EPS))
    wk1 = sb("wk1", [128, 1], F32)
    o_ = sb("o_", [128, D], F32)
    xn_tm = sb("xn_tm", [128, D], F32)
    cone = sb("cone", [128, 1], F32)
    p0 = ps_("p0", [128, 512], F32)
    p1 = ps_("p1", [128, 512], F32)
    p2 = ps_("p2", [128, 512], F32)
    p3 = ps_("p3", [128, 512], F32)

    em.dma(idn_f[:], idn[:, :])
    em.op("gpsimd", lambda e: e.memset(ones_r[0:1, :], 1.0))
    em.op("vector", lambda e: e.memset(hstate[:], 0.0))
    em.op("vector", lambda e: e.memset(cone[:, 0:1], 1.0))
    em.op("vector", lambda e: e.memset(xc_bf[:], 0.0))

    def prep_transpose(src, rows, cols, dstT):
        for rtile in range(rows // 128):
            for ch in range(0, cols, 1024):
                cw_ = min(1024, cols - ch)
                em.dma(stg[:, 0:cw_], src[rtile * 128:(rtile + 1) * 128, ch:ch + cw_])
                for g in range(cw_ // 128):
                    em.op("tensor", lambda e, g=g: e.transpose(p0[:, 0:128], stg[:, g * 128:(g + 1) * 128], idn_f[:]))
                    em.op("vector", lambda e: e.tensor_copy(tbf[:], p0[:, 0:128]))
                    em.dma(dstT[ch + g * 128:ch + (g + 1) * 128, rtile * 128:(rtile + 1) * 128], tbf[:])

    prep_transpose(ipw, 2 * DI, D, ipT)
    prep_transpose(dpw, DI, DI, dpT)
    prep_transpose(opw, D, DI, opT)
    for ch in range(2):
        em.dma(stg[0:32, 0:1024], xpw[:, ch * 1024:(ch + 1) * 1024])
        for g in range(8):
            em.op("tensor", lambda e, g=g: e.transpose(p0[:, 0:32], stg[0:32, g * 128:(g + 1) * 128], idn_f[0:32, 0:32]))
            em.op("vector", lambda e, g=g, ch=ch: e.tensor_copy(xps_bf[:, (ch * 8 + g) * 32:(ch * 8 + g + 1) * 32], p0[:, 0:32]))

    def bcast_row(dst, src_row1xn, width):
        em.op("vector", lambda e: e.tensor_copy(rowb[0:1, 0:width], src_row1xn))
        nchunks = (width + 511) // 512
        for q in range(nchunks):
            w = min(512, width - q * 512)
            em.op("tensor", lambda e, q=q, w=w: e.matmul(p0[:, 0:w], ones_r[0:1, 0:128], rowb[0:1, q * 512:q * 512 + w], start=True, stop=True))
            em.op("vector", lambda e, q=q, w=w, dst=dst: e.tensor_copy(dst[:, q * 512:q * 512 + w], p0[:, 0:w]))

    em.dma(rowf[0:1, :], nw[:].rearrange("(o c) -> o c", o=1))
    bcast_row(nw_bc, rowf[0:1, 0:D], D)
    em.dma(rowf[0:1, :], nb[:].rearrange("(o c) -> o c", o=1))
    bcast_row(nb_bc, rowf[0:1, 0:D], D)
    em.dma(rowf[0:1, :], opb[:].rearrange("(o c) -> o c", o=1))
    bcast_row(ob_bc, rowf[0:1, 0:D], D)
    em.dma_group([(ipb_sb[:, m:m + 1], ipb[m * 128:(m + 1) * 128].rearrange("(p o) -> p o", o=1)) for m in range(32)])
    em.dma_group([(cw_sb.rearrange("p (g k) -> p g k", k=4)[:, g, :], cw[g * 128:(g + 1) * 128, 0, :]) for g in range(DIG)])
    em.dma_group([(cb_sb[:, g:g + 1], cb[g * 128:(g + 1) * 128].rearrange("(p o) -> p o", o=1)) for g in range(DIG)])
    em.dma(xpb_sb[0:32, 0:1], xpb[:].rearrange("(p o) -> p o", o=1))
    em.dma_group([(dpb_sb[:, g:g + 1], dpb[g * 128:(g + 1) * 128].rearrange("(p o) -> p o", o=1)) for g in range(DIG)])
    em.dma_group([(dp_sb[:, g:g + 1], dpar[g * 128:(g + 1) * 128].rearrange("(p o) -> p o", o=1)) for g in range(DIG)])
    em.dma_group([(an_sb[:, g * DS:(g + 1) * DS], alog[g * 128:(g + 1) * 128, :]) for g in range(DIG)])
    em.op("scalar", lambda e: e.activation(an_sb[:], an_sb[:], AF.Exp))
    em.op("vector", lambda e: e.tensor_scalar(an_sb[:], an_sb[:], -1.0, None, op0=ALU.mult))

    ipT_r = ipT.rearrange("(g p) m -> p g m", p=128)
    dpT_r = dpT.rearrange("(g p) m -> p g m", p=128)
    opT_r = opT.rearrange("(g p) m -> p g m", p=128)

    for c in range(NCH):
        for ts in range(CH // 128):
            tt = c * 4 + ts
            em.dma(xres[:], x2[tt * 128:(tt + 1) * 128, :])
            em.op("vector", lambda e: e.tensor_copy(s_tm[:], xres[:]))
            emit_ln(em, s_tm, xn_tm, wk1, nsum, sqt, ceps)
            em.op("vector", lambda e: e.tensor_tensor(xn_tm[:], xn_tm[:], nw_bc[:], op=ALU.mult))
            em.op("vector", lambda e: e.tensor_tensor(xn_tm[:], xn_tm[:], nb_bc[:], op=ALU.add))
            for g in range(DG):
                em.op("tensor", lambda e, g=g: e.transpose(p0[:, 0:128], xn_tm[:, g * 128:(g + 1) * 128], idn_f[:]))
                em.op("vector", lambda e, g=g, ts=ts: e.tensor_copy(xnT[:, g * CH + ts * 128: g * CH + (ts + 1) * 128], p0[:, 0:128]))
        for m in range(32):
            em.dma(wstr[:, 0:DG * 128].rearrange("p (g m) -> p g m", g=DG), ipT_r[:, :, m * 128:(m + 1) * 128])
            em.group("tensor", [
                (lambda e, kg=kg: e.matmul(p1[:, 0:CH], wstr[:, kg * 128:(kg + 1) * 128],
                                           xnT[:, kg * CH:(kg + 1) * CH], start=(kg == 0), stop=(kg == DG - 1)))
                for kg in range(DG)])
            if m < 16:
                em.op("vector", lambda e, m=m: e.tensor_scalar(
                    xc_bf.rearrange("p (g t) -> p g t", t=CH + 3)[:, m, 3:CH + 3],
                    p1[:, 0:CH], ipb_sb[:, m:m + 1], None, op0=ALU.add))
            else:
                em.op("vector", lambda e, m=m: e.tensor_scalar(
                    z_bf.rearrange("p (g t) -> p g t", t=CH)[:, m - 16, :],
                    p1[:, 0:CH], ipb_sb[:, m:m + 1], None, op0=ALU.add))
        xc_r = xc_bf.rearrange("p (g t) -> p g t", t=CH + 3)
        u_r = u_bf.rearrange("p (g t) -> p g t", t=CH)
        for g in range(DIG):
            cwr = cw_sb.rearrange("p (g k) -> p g k", k=4)
            em.op("vector", lambda e, g=g: e.tensor_scalar(sp1[:], xc_r[:, g, 0:CH], cwr[:, g, 0:1], None, op0=ALU.mult))
            for k in range(1, 4):
                em.op("vector", lambda e, g=g, k=k: e.scalar_tensor_tensor(
                    sp1[:], xc_r[:, g, k:k + CH], cwr[:, g, k:k + 1], sp1[:], op0=ALU.mult, op1=ALU.add))
            em.op("scalar", lambda e, g=g: e.activation(u_r[:, g, :], sp1[:], AF.Silu, bias=cb_sb[:, g:g + 1]))
            em.op("vector", lambda e, g=g: e.tensor_copy(xc_r[:, g, 0:3], xc_r[:, g, CH:CH + 3]))
        em.group("tensor", [
            (lambda e, kg=kg: e.matmul(p1[0:32, 0:CH], xps_bf[:, kg * 32:(kg + 1) * 32],
                                       u_r[:, kg, :], start=(kg == 0), stop=(kg == DIG - 1)))
            for kg in range(DIG)])
        em.op("vector", lambda e: e.tensor_scalar(xdb[0:32, :], p1[0:32, 0:CH], xpb_sb[0:32, 0:1], None, op0=ALU.add))
        for s in range(DS):
            for (src_row, dst) in ((s, bcB), (16 + s, bcC)):
                em.dma(rws[0:1, :], xdb[src_row:src_row + 1, :])
                em.op("vector", lambda e: e.tensor_copy(rwb[0:1, 0:CH], rws[0:1, :]))
                em.op("tensor", lambda e: e.matmul(p2[:, 0:CH], ones_r[0:1, 0:128], rwb[0:1, 0:CH], start=True, stop=True))
                em.op("vector", lambda e, s=s, dst=dst: e.tensor_copy(
                    dst.rearrange("p (s t) -> p s t", t=CH)[:, s, :], p2[:, 0:CH]))
        d_r = delta.rearrange("p (g t) -> p g t", t=CH)
        du_r = du_bf.rearrange("p (g t) -> p g t", t=CH)
        for g in range(DIG):
            em.dma(wstr[:].rearrange("p (g m) -> p g m", g=DIG)[:, 0:8, :], dpT_r[:, 0:8, g * 128:(g + 1) * 128])
            em.dma(wstr[:].rearrange("p (g m) -> p g m", g=DIG)[:, 8:16, :], dpT_r[:, 8:16, g * 128:(g + 1) * 128])
            em.group("tensor", [
                (lambda e, kg=kg: e.matmul(p1[:, 0:CH], wstr[:, kg * 128:(kg + 1) * 128],
                                           u_r[:, kg, :], start=(kg == 0), stop=(kg == DIG - 1)))
                for kg in range(DIG)])
            em.op("vector", lambda e, g=g: e.tensor_scalar(sp0[:], p1[:, 0:CH], dpb_sb[:, g:g + 1], None, op0=ALU.add))
            em.op("vector", lambda e: e.tensor_scalar(sp1[:], sp0[:], 0.0, None, op0=ALU.max))
            em.op("scalar", lambda e: e.activation(sp2[:], sp0[:], AF.Abs))
            em.op("scalar", lambda e: e.activation(sp2[:], sp2[:], AF.Exp, scale=-1.0))
            em.op("scalar", lambda e: e.activation(sp2[:], sp2[:], AF.Ln, bias=cone[:, 0:1]))
            em.op("vector", lambda e, g=g: e.tensor_tensor(d_r[:, g, :], sp1[:], sp2[:], op=ALU.add))
            em.op("vector", lambda e, g=g: e.tensor_tensor(du_r[:, g, :], d_r[:, g, :], u_r[:, g, :], op=ALU.mult))
        bcB_r = bcB.rearrange("p (s t) -> p s t", t=CH)
        bcC_r = bcC.rearrange("p (s t) -> p s t", t=CH)
        y_r = ybf.rearrange("p (g t) -> p g t", t=CH)
        for g in range(DIG):
            for s in range(DS):
                em.op("scalar", lambda e, g=g, s=s: e.activation(dab[:], d_r[:, g, :], AF.Exp, scale=an_sb[:, g * DS + s: g * DS + s + 1]))
                em.op("vector", lambda e, g=g, s=s: e.tensor_tensor(dbu[:], du_r[:, g, :], bcB_r[:, s, :], op=ALU.mult))
                em.op("vector", lambda e, g=g, s=s: e.tensor_tensor_scan(
                    hsc[:], dab[:], dbu[:], hstate[:, g * DS + s: g * DS + s + 1], op0=ALU.mult, op1=ALU.add))
                em.op("vector", lambda e, g=g, s=s: e.tensor_copy(hstate[:, g * DS + s: g * DS + s + 1], hsc[:, CH - 1:CH]))
                if s == 0:
                    em.op("vector", lambda e, g=g, s=s: e.tensor_tensor(yg[:], hsc[:], bcC_r[:, s, :], op=ALU.mult))
                else:
                    em.op("vector", lambda e, g=g, s=s: e.tensor_tensor(ytmp[:], hsc[:], bcC_r[:, s, :], op=ALU.mult))
                    em.op("vector", lambda e: e.tensor_tensor(yg[:], yg[:], ytmp[:], op=ALU.add))
            em.op("vector", lambda e, g=g: e.scalar_tensor_tensor(yg[:], u_r[:, g, :], dp_sb[:, g:g + 1], yg[:], op0=ALU.mult, op1=ALU.add))
            em.op("scalar", lambda e, g=g: e.activation(zsil[:], z_bf.rearrange("p (g t) -> p g t", t=CH)[:, g, :], AF.Silu))
            em.op("vector", lambda e, g=g: e.tensor_tensor(y_r[:, g, :], yg[:], zsil[:], op=ALU.mult))
        du_st = du_bf.rearrange("p (a b) -> p a b", a=16)
        for nn in range(2):
            for q in range(4):
                em.dma(otst[:].rearrange("p (g m) -> p g m", g=DIG)[:, q * 4:(q + 1) * 4, :],
                       opT_r[:, q * 4:(q + 1) * 4, nn * 512:(nn + 1) * 512])
            for ts in range(4):
                em.group("tensor", [
                    (lambda e, kg=kg, ts=ts: e.matmul(p3[:, 0:512],
                                                      y_r[:, kg, ts * 128:(ts + 1) * 128],
                                                      otst[:, kg * 512:(kg + 1) * 512],
                                                      start=(kg == 0), stop=(kg == DIG - 1)))
                    for kg in range(DIG)])
                em.op("vector", lambda e, ts=ts, nn=nn: e.tensor_tensor(
                    du_st[:, ts * 2 + nn, 0:512], p3[:, 0:512],
                    ob_bc[:, nn * 512:(nn + 1) * 512], op=ALU.add))
        for ts in range(4):
            tt = c * 4 + ts
            em.dma(xres[:], x2[tt * 128:(tt + 1) * 128, :])
            for nn in range(2):
                em.op("vector", lambda e, ts=ts, nn=nn: e.scalar_tensor_tensor(
                    s_tm[:, nn * 512:(nn + 1) * 512], xres[:, nn * 512:(nn + 1) * 512], 2.0,
                    du_st[:, ts * 2 + nn, 0:512], op0=ALU.mult, op1=ALU.add))
            emit_ln(em, s_tm, o_, wk1, nsum, sqt, ceps)
            em.dma(x3[tt * 128:(tt + 1) * 128, :], o_[:])
    st.close()


def emit_l4(nc, em, x3, rW, rb, sW1, sb1, sW2, sb2, eW1, eb1, eW2, eb2, idn, out):
    P = "g_"
    E = 15
    H = 1024
    st = ExitStack()
    sb_ = lambda nm, sh, dt: st.enter_context(nc.sbuf_tensor(P + nm, sh, dt))
    ps_ = lambda nm, sh, dt: st.enter_context(nc.psum_tensor(P + nm, sh, dt))

    idn_f = sb_("idn_f", [128, 128], F32)
    ones_r = sb_("ones_r", [1, 128], BF16)
    xs = sb_("xs", [128, D], F32)
    x3T = sb_("x3T", [128, DG * 512], BF16)
    rwbf = sb_("rwbf", [128, DG * E], BF16)
    brb = sb_("brb", [128, E], F32)
    sw1bf = sb_("sw1bf", [128, DG * H], BF16)
    sw2bf = sb_("sw2bf", [128, DG * D], BF16)
    sb1p = sb_("sb1p", [128, 8], F32)
    sb2bc = sb_("sb2bc", [128, D], F32)
    eb1p = sb_("eb1p", [128, E * 8], F32)
    eb2bc = sb_("eb2bc", [128, D], F32)
    rowf = sb_("rowf", [1, 1024], F32)
    rowb = sb_("rowb", [1, 1024], BF16)
    wst = sb_("wst", [128, 1024], F32)
    wbf = sb_("wbf", [128, 1024], BF16)
    hsh = sb_("hsh", [128, 8 * 512], BF16)
    he = sb_("he", [128, 8 * 512], BF16)
    Wr = sb_("Wr", [128, 4 * 16], F32)
    pp = sb_("pp", [128, E], F32)
    ge1 = sb_("ge1", [128, E], F32)
    m1 = sb_("m1", [128, 1], F32)
    m2 = sb_("m2", [128, 1], F32)
    acc = sb_("acc", [128, 4 * D], F32)
    tz = sb_("tz", [128, 512], F32)
    xres = sb_("xres", [128, D], F32)
    sqt = sb_("sqt", [128, D], F32)
    nsum = sb_("nsum", [128, 1], F32)
    ceps = sb_("ceps", [128, 1], F32)
    _ = em.op("vector", lambda e: e.memset(ceps[:, 0:1], EPS))
    wk1 = sb_("wk1", [128, 1], F32)
    o_ = sb_("o_", [128, D], F32)
    pz = [ps_("pz%d" % i, [128, 512], F32) for i in range(8)]

    em.dma(idn_f[:], idn[:, :])
    em.op("gpsimd", lambda e: e.memset(ones_r[0:1, :], 1.0))

    def bcast_row(dst, width):
        em.op("vector", lambda e: e.tensor_copy(rowb[0:1, 0:width], rowf[0:1, 0:width]))
        for q in range((width + 511) // 512):
            w = min(512, width - q * 512)
            em.op("tensor", lambda e, q=q, w=w: e.matmul(pz[6][:, 0:w], ones_r[0:1, 0:128], rowb[0:1, q * 512:q * 512 + w], start=True, stop=True))
            em.op("vector", lambda e, q=q, w=w, dst=dst: e.tensor_copy(dst[:, q * 512:q * 512 + w], pz[6][:, 0:w]))

    for g in range(DG):
        em.dma(wst[:, 0:E], rW[g * 128:(g + 1) * 128, :])
        em.op("vector", lambda e, g=g: e.tensor_copy(rwbf[:, g * E:(g + 1) * E], wst[:, 0:E]))
    em.dma(rowf[0:1, 0:E], rb[:].rearrange("(o c) -> o c", o=1))
    bcast_row(brb, E)
    em.dma(rowf[0:1, :], sb2[:].rearrange("(o c) -> o c", o=1))
    bcast_row(sb2bc, D)
    em.dma_group([(sb1p[:, m:m + 1], sb1[m * 128:(m + 1) * 128].rearrange("(p o) -> p o", o=1)) for m in range(8)])
    for e_ in range(E):
        em.dma_group([(eb1p[:, e_ * 8 + m: e_ * 8 + m + 1], eb1[e_, m * 128:(m + 1) * 128].rearrange("(p o) -> p o", o=1)) for m in range(8)])
    for kg in range(DG):
        em.dma(wst[:], sW1[kg * 128:(kg + 1) * 128, :])
        em.op("vector", lambda e, kg=kg: e.tensor_copy(sw1bf[:, kg * H:(kg + 1) * H], wst[:]))
        em.dma(wst[:], sW2[kg * 128:(kg + 1) * 128, :])
        em.op("vector", lambda e, kg=kg: e.tensor_copy(sw2bf[:, kg * D:(kg + 1) * D], wst[:]))

    for c in range(8):
        for ts in range(4):
            tt = c * 4 + ts
            em.dma(xs[:], x3[tt * 128:(tt + 1) * 128, :])
            for g in range(DG):
                em.op("tensor", lambda e, g=g: e.transpose(pz[7][:, 0:128], xs[:, g * 128:(g + 1) * 128], idn_f[:]))
                em.op("vector", lambda e, g=g, ts=ts: e.tensor_copy(x3T[:, g * 512 + ts * 128: g * 512 + (ts + 1) * 128], pz[7][:, 0:128]))
            em.group("tensor", [
                (lambda e, g=g, ts=ts: e.matmul(pz[0][:, 0:E], x3T[:, g * 512 + ts * 128: g * 512 + ts * 128 + 128],
                                                rwbf[:, g * E:(g + 1) * E], start=(g == 0), stop=(g == DG - 1)))
                for g in range(DG)])
            em.op("vector", lambda e: e.tensor_tensor(pp[:], pz[0][:, 0:E], brb[:], op=ALU.add))
            em.op("vector", lambda e: e.reduce_max(m1[:, 0:1], pp[:], axis=AX.X, negate=True))
            em.op("scalar", lambda e: e.activation(pp[:], pp[:], AF.Exp, bias=m1[:, 0:1], accum_out=nsum[:, 0:1]))
            em.op("vector", lambda e: e.reciprocal(wk1[:, 0:1], nsum[:, 0:1]))
            em.op("vector", lambda e: e.tensor_scalar(pp[:], pp[:], wk1[:, 0:1], None, op0=ALU.mult))
            em.op("vector", lambda e: e.reduce_max(m1[:, 0:1], pp[:], axis=AX.X))
            em.op("vector", lambda e: e.tensor_scalar(ge1[:], pp[:], m1[:, 0:1], None, op0=ALU.is_ge))
            em.op("vector", lambda e: e.scalar_tensor_tensor(ge1[:], ge1[:], -2.0, pp[:], op0=ALU.mult, op1=ALU.add))
            em.op("vector", lambda e: e.reduce_max(m2[:, 0:1], ge1[:], axis=AX.X))
            em.op("vector", lambda e: e.tensor_scalar(sqt[:, 0:E], ge1[:], m2[:, 0:1], None, op0=ALU.is_ge))
            em.op("vector", lambda e: e.scalar_tensor_tensor(ge1[:], sqt[:, 0:E], -2.0, ge1[:], op0=ALU.mult, op1=ALU.add))
            em.op("vector", lambda e: e.reduce_max(m2[:, 0:1], ge1[:], axis=AX.X))
            em.op("vector", lambda e: e.tensor_scalar(ge1[:], pp[:], m2[:, 0:1], None, op0=ALU.is_ge))
            em.op("vector", lambda e, ts=ts: e.tensor_tensor(Wr[:, ts * 16: ts * 16 + E], pp[:], ge1[:], op=ALU.mult))
        for m in range(8):
            em.group("tensor", [
                (lambda e, kg=kg, m=m: e.matmul(pz[0][:, 0:512], sw1bf[:, kg * H + m * 128: kg * H + (m + 1) * 128],
                                                x3T[:, kg * 512:(kg + 1) * 512], start=(kg == 0), stop=(kg == DG - 1)))
                for kg in range(DG)])
            em.op("scalar", lambda e, m=m: e.activation(hsh[:, m * 512:(m + 1) * 512], pz[0][:, 0:512], AF.Relu, bias=sb1p[:, m:m + 1]))
        fns = []
        for kg in range(8):
            for ts in range(4):
                for nn in range(2):
                    fns.append(lambda e, kg=kg, ts=ts, nn=nn: e.matmul(
                        pz[ts * 2 + nn][:, 0:512],
                        hsh[:, kg * 512 + ts * 128: kg * 512 + (ts + 1) * 128],
                        sw2bf[:, kg * D + nn * 512: kg * D + (nn + 1) * 512],
                        start=(kg == 0), stop=(kg == 7)))
        em.group("tensor", fns)
        for ts in range(4):
            for nn in range(2):
                em.op("vector", lambda e, ts=ts, nn=nn: e.tensor_tensor(
                    acc[:, ts * D + nn * 512: ts * D + (nn + 1) * 512],
                    pz[ts * 2 + nn][:, 0:512], sb2bc[:, nn * 512:(nn + 1) * 512], op=ALU.add))
        for e_ in range(E):
            em.dma(rowf[0:1, :], eb2[e_:e_ + 1, :])
            bcast_row(eb2bc, D)
            for m in range(8):
                em.dma(wst[:].rearrange("p (g m) -> p g m", g=DG), eW1[e_].rearrange("(g p) m -> p g m", p=128)[:, :, m * 128:(m + 1) * 128])
                em.op("vector", lambda e: e.tensor_copy(wbf[:], wst[:]))
                em.group("tensor", [
                    (lambda e, kg=kg: e.matmul(pz[0][:, 0:512], wbf[:, kg * 128:(kg + 1) * 128],
                                               x3T[:, kg * 512:(kg + 1) * 512], start=(kg == 0), stop=(kg == DG - 1)))
                    for kg in range(DG)])
                em.op("scalar", lambda e, m=m, e_=e_: e.activation(he[:, m * 512:(m + 1) * 512], pz[0][:, 0:512], AF.Relu, bias=eb1p[:, e_ * 8 + m: e_ * 8 + m + 1]))
            for kg in range(8):
                em.dma(wst[:], eW2[e_, kg * 128:(kg + 1) * 128, :])
                em.op("vector", lambda e: e.tensor_copy(wbf[:], wst[:]))
                fns = []
                for ts in range(4):
                    for nn in range(2):
                        fns.append(lambda e, kg=kg, ts=ts, nn=nn: e.matmul(
                            pz[ts * 2 + nn][:, 0:512],
                            he[:, kg * 512 + ts * 128: kg * 512 + (ts + 1) * 128],
                            wbf[:, nn * 512:(nn + 1) * 512],
                            start=(kg == 0), stop=(kg == 7)))
                em.group("tensor", fns)
            for ts in range(4):
                for nn in range(2):
                    em.op("vector", lambda e, ts=ts, nn=nn: e.tensor_tensor(
                        tz[:], pz[ts * 2 + nn][:, 0:512], eb2bc[:, nn * 512:(nn + 1) * 512], op=ALU.add))
                    em.op("vector", lambda e, ts=ts, nn=nn, e_=e_: e.scalar_tensor_tensor(
                        acc[:, ts * D + nn * 512: ts * D + (nn + 1) * 512], tz[:],
                        Wr[:, ts * 16 + e_: ts * 16 + e_ + 1],
                        acc[:, ts * D + nn * 512: ts * D + (nn + 1) * 512], op0=ALU.mult, op1=ALU.add))
        for ts in range(4):
            tt = c * 4 + ts
            em.dma(xres[:], x3[tt * 128:(tt + 1) * 128, :])
            em.op("vector", lambda e, ts=ts: e.tensor_tensor(xres[:], xres[:], acc[:, ts * D:(ts + 1) * D], op=ALU.add))
            emit_ln(em, xres, o_, wk1, nsum, sqt, ceps)
            em.dma(out[tt * 128:(tt + 1) * 128, :], o_[:])
    st.close()


def build_fused():
    nc = bass.Bass("TRN2", target_bir_lowering=False)
    d = lambda nm, sh: nc.dram_tensor(nm, sh, F32, kind="ExternalInput").ap()
    x = d("x", [N, D])
    wqkv = d("wqkv", [D, 3 * D])
    wproj = d("wproj", [D, D])
    cosq = d("cosq", [128, N])
    sinq = d("sinq", [128, N])
    cosk = d("cosk", [128, N])
    sink = d("sink", [128, N])
    rt = d("rt", [128, 128])
    idn = d("idn", [128, 128])
    f_rw = d("f_rw", [8, D])
    f_rb = d("f_rb", [8])
    f_lw = d("f_lw", [4, D])
    f_lb = d("f_lb", [4])
    f_w1 = d("f_w1", [8, D, 4096])
    f_b1 = d("f_b1", [8, 4096])
    f_w2 = d("f_w2", [8, 4096, D])
    f_b2 = d("f_b2", [8, D])
    m_nw = d("m_nw", [D])
    m_nb = d("m_nb", [D])
    m_ipw = d("m_ipw", [2 * DI, D])
    m_ipb = d("m_ipb", [2 * DI])
    m_cw = d("m_cw", [DI, 1, 4])
    m_cb = d("m_cb", [DI])
    m_xpw = d("m_xpw", [2 * DS, DI])
    m_xpb = d("m_xpb", [2 * DS])
    m_dpw = d("m_dpw", [DI, DI])
    m_dpb = d("m_dpb", [DI])
    m_alog = d("m_alog", [DI, DS])
    m_dpar = d("m_dpar", [DI])
    m_opw = d("m_opw", [D, DI])
    m_opb = d("m_opb", [D])
    g_rW = d("g_rW", [D, 15])
    g_rb = d("g_rb", [15])
    g_sW1 = d("g_sW1", [D, 1024])
    g_sb1 = d("g_sb1", [1024])
    g_sW2 = d("g_sW2", [1024, D])
    g_sb2 = d("g_sb2", [D])
    g_eW1 = d("g_eW1", [15, D, 1024])
    g_eb1 = d("g_eb1", [15, 1024])
    g_eW2 = d("g_eW2", [15, 1024, D])
    g_eb2 = d("g_eb2", [15, D])
    out = nc.dram_tensor("out", [N, D], F32, kind="ExternalOutput").ap()
    x1 = nc.dram_tensor("x1", [N, D], F32, kind="Internal").ap()
    x2 = nc.dram_tensor("x2", [N, D], F32, kind="Internal").ap()
    x3 = nc.dram_tensor("x3", [N, D], F32, kind="Internal").ap()

    em = Em()
    emit_l1(nc, em, x, wqkv, wproj, cosq, sinq, cosk, sink, rt, idn, x1)
    emit_l2(nc, em, x1, f_rw, f_rb, f_lw, f_lb, f_w1, f_b1, f_w2, f_b2, idn, x2)
    emit_l3(nc, em, x2, m_nw, m_nb, m_ipw, m_ipb, m_cw, m_cb, m_xpw, m_xpb, m_dpw, m_dpb,
            m_alog, m_dpar, m_opw, m_opb, idn, x3)
    emit_l4(nc, em, x3, g_rW, g_rb, g_sW1, g_sb1, g_sW2, g_sb2, g_eW1, g_eb1, g_eW2, g_eb2, idn, out)
    with nc.semaphore("tok") as tok:
        em.build(nc, tok)
    return nc


_CACHE = {}


def _get(name, builder):
    if name not in _CACHE:
        _CACHE[name] = builder()
    return _CACHE[name]


def kernel(**inp):
    cosq, sinq, cosk, sink, rt, idn = host_consts()
    cores = list(range(8))
    x_full = np.ascontiguousarray(inp["L0_attn_x"].astype(np.float32))
    nc = _get("fused", build_fused)
    maps = [dict(
        x=np.ascontiguousarray(x_full[c]), wqkv=inp["L0_attn_w_qkv"], wproj=inp["L0_attn_w_proj"],
        cosq=cosq, sinq=sinq, cosk=cosk, sink=sink, rt=rt, idn=idn,
        f_rw=inp["L0_ffn_router_w"], f_rb=inp["L0_ffn_router_b"], f_lw=inp["L0_ffn_lang_w"],
        f_lb=inp["L0_ffn_lang_b"], f_w1=inp["L0_ffn_w1"], f_b1=inp["L0_ffn_b1"],
        f_w2=inp["L0_ffn_w2"], f_b2=inp["L0_ffn_b2"],
        m_nw=inp["L1_attn_norm_w"], m_nb=inp["L1_attn_norm_b"], m_ipw=inp["L1_attn_in_proj_w"],
        m_ipb=inp["L1_attn_in_proj_b"], m_cw=inp["L1_attn_conv_w"], m_cb=inp["L1_attn_conv_b"],
        m_xpw=inp["L1_attn_x_proj_w"], m_xpb=inp["L1_attn_x_proj_b"], m_dpw=inp["L1_attn_dt_proj_w"],
        m_dpb=inp["L1_attn_dt_proj_b"], m_alog=inp["L1_attn_A_log"], m_dpar=inp["L1_attn_D_param"],
        m_opw=inp["L1_attn_out_proj_w"], m_opb=inp["L1_attn_out_proj_b"],
        g_rW=inp["L1_ffn_router_W"], g_rb=inp["L1_ffn_router_b"], g_sW1=inp["L1_ffn_sW1"],
        g_sb1=inp["L1_ffn_sb1"], g_sW2=inp["L1_ffn_sW2"], g_sb2=inp["L1_ffn_sb2"],
        g_eW1=inp["L1_ffn_eW1"], g_eb1=inp["L1_ffn_eb1"], g_eW2=inp["L1_ffn_eW2"],
        g_eb2=inp["L1_ffn_eb2"]) for c in cores]
    r = run_bass_kernel_spmd(nc, maps, core_ids=cores)
    outs = [r.results[c]["out"] for c in cores]
    return np.stack(outs, axis=0).astype(np.float32)


# version 06
# speedup vs baseline: 1.2059x; 1.2059x over previous
import numpy as np
import ml_dtypes
import concourse.bass as bass
import concourse.mybir as mybir
from concourse.bass_utils import run_bass_kernel_spmd
from contextlib import ExitStack

F32 = mybir.dt.float32
BF16 = mybir.dt.bfloat16
AF = mybir.ActivationFunctionType
ALU = mybir.AluOpType
AX = mybir.AxisListType

ENGINES = ("tensor", "vector", "scalar", "sync", "gpsimd")

N = 4096
D = 1024
TT = N // 128
DG = D // 128
HEADS = 16
HD = 64
DI = 2048
DIG = DI // 128
DS = 16
EPS = 1e-5


class Em:
    def __init__(self):
        self.items = []
        self.val = 0

    def op(self, eng, fn, inc=1):
        self.items.append((eng, self.val, inc, fn))
        self.val += inc

    def group(self, eng, fns, inc=1):
        w = self.val
        for i, fn in enumerate(fns):
            self.items.append((eng, w if i == 0 else None, inc, fn))
            self.val += inc

    def stage(self, entries):
        w = self.val
        for ent in entries:
            eng, fn = ent[0], ent[1]
            inc = ent[2] if len(ent) > 2 else (16 if eng == "sync" else 1)
            self.items.append((eng, w, inc, fn))
            self.val += inc
        return self.val

    def dma(self, dst, src, eng="sync"):
        self.op(eng, lambda e, d=dst, s=src: e.dma_start(d, s), inc=16)

    def dma_group(self, pairs, eng="sync"):
        self.group(eng, [lambda e, d=d_, s=s_: e.dma_start(d, s) for (d_, s_) in pairs], inc=16)

    def build(self, nc, tok):
        final = self.val
        with nc.Block() as block:
            for eng_name in ENGINES:
                ops = [it for it in self.items if it[0] == eng_name]
                if not ops and eng_name != "sync":
                    continue

                def section(engine, ops=ops, eng_name=eng_name):
                    for (_, wait, inc, fn) in ops:
                        if wait is not None and wait > 0:
                            engine.wait_ge(tok, wait)
                        inst = fn(engine)
                        if inc:
                            inst.then_inc(tok, inc)
                    if eng_name == "sync":
                        engine.wait_ge(tok, final)

                getattr(block, eng_name)(section)


def emit_ln(em, s, out, work, nsum, sqt, ceps):
    em.op("vector", lambda e: e.tensor_reduce(nsum[:, 0:1], s[:], axis=AX.X, op=ALU.add, negate=True))
    em.op("scalar", lambda e: e.activation(work[:, 0:1], nsum[:, 0:1], AF.Copy, scale=1.0 / D))
    em.op("vector", lambda e: e.tensor_scalar(s[:], s[:], work[:, 0:1], None, op0=ALU.add))
    em.op("scalar", lambda e: e.activation(sqt[:], s[:], AF.Square, accum_out=nsum[:, 0:1]))
    em.op("scalar", lambda e: e.activation(work[:, 0:1], nsum[:, 0:1], AF.Ln, scale=1.0 / D, bias=ceps[:, 0:1]))
    em.op("scalar", lambda e: e.activation(work[:, 0:1], work[:, 0:1], AF.Exp, scale=-0.5))
    em.op("vector", lambda e: e.tensor_scalar(out[:], s[:], work[:, 0:1], None, op0=ALU.mult))


def host_consts():
    inv = 1.0 / (10000.0 ** (np.arange(0, HD, 2, dtype=np.float64) / HD))
    freqs = np.arange(N, dtype=np.float64)[:, None] * inv[None, :]
    cosv = np.cos(freqs)
    sinv = np.sin(freqs)
    fidx = (np.arange(128) % 64) // 2
    cosk = cosv[:, fidx].T.astype(np.float32).copy()
    sink = sinv[:, fidx].T.astype(np.float32).copy()
    cosq = (0.125 * cosk).astype(np.float32)
    sinq = (0.125 * sink).astype(np.float32)
    rt = np.zeros((128, 128), np.float32)
    for b in (0, 64):
        for i in range(32):
            rt[b + 2 * i + 1, b + 2 * i] = -1.0
            rt[b + 2 * i, b + 2 * i + 1] = 1.0
    idn = np.eye(128, dtype=np.float32)
    return cosq, sinq, cosk, sink, rt, idn


def emit_l1(nc, em, x, wqkv, wproj, cosq, sinq, cosk, sink, rt, idn, x1):
    P = "a_"
    qf = nc.dram_tensor(P + "qf", [D, N], BF16, kind="Internal").ap()
    kf = nc.dram_tensor(P + "kf", [D, N], BF16, kind="Internal").ap()
    vf = nc.dram_tensor(P + "vf", [N, D], BF16, kind="Internal").ap()
    at = nc.dram_tensor(P + "at", [D, N], BF16, kind="Internal").ap()

    st = ExitStack()
    sb = lambda nm, sh, dt: st.enter_context(nc.sbuf_tensor(P + nm, sh, dt))
    ps = lambda nm, sh, dt: st.enter_context(nc.psum_tensor(P + nm, sh, dt))

    idn_f = sb("idn_f", [128, 128], F32)
    idn_b = sb("idn_b", [128, 128], BF16)
    rt_bf = sb("rt_bf", [128, 128], BF16)
    stage1 = sb("stage1", [128, 128], F32)
    xs = sb("xs", [128, D], F32)
    ones_b = sb("ones_b", [128, 1], BF16)
    ones_r = sb("ones_r", [1, 128], BF16)
    ps0 = ps("ps0", [128, 512], F32)
    ps1 = ps("ps1", [128, 512], F32)

    em.dma_group([(idn_f[:], idn[:, :]), (stage1[:], rt[:, :])])
    em.op("vector", lambda e: e.tensor_copy(idn_b[:], idn_f[:]))
    em.op("vector", lambda e: e.tensor_copy(rt_bf[:], stage1[:]))
    em.op("gpsimd", lambda e: e.memset(ones_b[:, 0:1], 1.0))
    em.op("gpsimd", lambda e: e.memset(ones_r[0:1, :], 1.0))

    stA = ExitStack()
    sbA = lambda nm, sh, dt: stA.enter_context(nc.sbuf_tensor(P + nm, sh, dt))
    xT = sbA("xT", [128, DG * N], BF16)
    cq = sbA("cq", [128, N], F32)
    sq_ = sbA("sq_", [128, N], F32)
    ck = sbA("ck", [128, N], F32)
    sk = sbA("sk", [128, N], F32)
    wst = sbA("wst", [128, DG * 128], F32)
    wbf = sbA("wbf", [128, DG * 128], BF16)
    qkb = sbA("qkb", [128, 512], BF16)
    t1 = sbA("t1", [128, 512], F32)
    t2 = sbA("t2", [128, 512], F32)
    obf = sbA("obf", [128, 512], BF16)
    vbt = sbA("vbt", [128, 128], BF16)

    em.dma_group([(cq[:], cosq[:, :]), (sq_[:], sinq[:, :]), (ck[:], cosk[:, :]), (sk[:], sink[:, :])])

    for tt in range(TT):
        em.dma(xs[:], x[tt * 128:(tt + 1) * 128, :])
        for g in range(DG):
            em.op("tensor", lambda e, g=g: e.transpose(ps0[:, 0:128], xs[:, g * 128:(g + 1) * 128], idn_f[:]))
            em.op("vector", lambda e, g=g, tt=tt: e.tensor_copy(xT[:, g * N + tt * 128: g * N + tt * 128 + 128], ps0[:, 0:128]))

    wqkv_r = wqkv.rearrange("(g p) m -> p g m", p=128)
    for m in range(16):
        em.dma(wst.rearrange("p (g m) -> p g m", g=DG)[:, :, :], wqkv_r[:, :, m * 128:(m + 1) * 128])
        em.op("vector", lambda e: e.tensor_copy(wbf[:], wst[:]))
        is_q = m < 8
        ctab = cq if is_q else ck
        stab = sq_ if is_q else sk
        dst = qf if is_q else kf
        row = (m if is_q else m - 8) * 128
        for n in range(8):
            em.group("tensor", [
                (lambda e, kg=kg, n=n: e.matmul(ps0[:, 0:512], wbf[:, kg * 128:(kg + 1) * 128],
                                                xT[:, kg * N + n * 512: kg * N + (n + 1) * 512],
                                                start=(kg == 0), stop=(kg == DG - 1)))
                for kg in range(DG)])
            em.op("vector", lambda e: e.tensor_copy(qkb[:], ps0[:, 0:512]))
            em.op("tensor", lambda e: e.matmul(ps1[:, 0:512], rt_bf[:], qkb[:], start=True, stop=True))
            em.op("vector", lambda e, n=n, ctab=ctab: e.tensor_tensor(t1[:], ps0[:, 0:512], ctab[:, n * 512:(n + 1) * 512], op=ALU.mult))
            em.op("vector", lambda e, n=n, stab=stab: e.tensor_tensor(t2[:], ps1[:, 0:512], stab[:, n * 512:(n + 1) * 512], op=ALU.mult))
            em.op("vector", lambda e: e.tensor_tensor(obf[:], t1[:], t2[:], op=ALU.add))
            em.dma(dst[row:row + 128, n * 512:(n + 1) * 512], obf[:])
    for m in range(8):
        em.dma(wst.rearrange("p (g m) -> p g m", g=DG)[:, :, :], wqkv_r[:, :, (16 + m) * 128:(17 + m) * 128])
        em.op("vector", lambda e: e.tensor_copy(wbf[:], wst[:]))
        for tt in range(TT):
            em.group("tensor", [
                (lambda e, kg=kg, tt=tt: e.matmul(ps0[:, 0:128],
                                                  xT[:, kg * N + tt * 128: kg * N + (tt + 1) * 128],
                                                  wbf[:, kg * 128:(kg + 1) * 128],
                                                  start=(kg == 0), stop=(kg == DG - 1)))
                for kg in range(DG)])
            em.op("vector", lambda e: e.tensor_copy(vbt[:], ps0[:, 0:128]))
            em.dma(vf[tt * 128:(tt + 1) * 128, m * 128:(m + 1) * 128], vbt[:])
    stA.close()

    stB = ExitStack()
    sbB = lambda nm, sh, dt: stB.enter_context(nc.sbuf_tensor(P + nm, sh, dt))
    qh = sbB("qh", [64, N], BF16)
    kh = sbB("kh", [64, N], BF16)
    vh = sbB("vh", [128, TT * 64], BF16)
    est = sbB("est", [128, TT * 512], BF16)
    rr = sbB("rr", [1, 512], F32)
    rrb = sbB("rrb", [1, 512], BF16)
    bcs = sbB("bcs", [64, 512], F32)
    ybf = sbB("ybf", [64, 512], BF16)
    psST = ps("psST", [128, 512], F32)
    psYT = ps("psYT", [128, 512], F32)
    psR1 = ps("psR1", [128, 512], F32)
    psBC = ps("psBC", [128, 512], F32)

    vf_r = vf.rearrange("(tt p) c -> p tt c", p=128)
    for h in range(HEADS):
        em.dma(qh[:, :], qf[h * 64:(h + 1) * 64, :])
        em.dma(kh[:, :], kf[h * 64:(h + 1) * 64, :])
        em.dma_group([
            (vh.rearrange("p (tt c) -> p tt c", c=64)[:, q * 8:(q + 1) * 8, :],
             vf_r[:, q * 8:(q + 1) * 8, h * 64:(h + 1) * 64])
            for q in range(4)])
        for tqn in range(8):
            for tkb in range(TT):
                em.op("tensor", lambda e, tkb=tkb, tqn=tqn: e.matmul(
                    psST[:, 0:512], kh[:, tkb * 128:(tkb + 1) * 128],
                    qh[:, tqn * 512:(tqn + 1) * 512], start=True, stop=True))
                em.op("scalar", lambda e, tkb=tkb: e.activation(
                    est[:, tkb * 512:(tkb + 1) * 512], psST[:, 0:512], AF.Exp))
            fns = []
            for tkb in range(TT):
                fns.append(lambda e, tkb=tkb: e.matmul(
                    psYT[0:64, 0:512], vh[:, tkb * 64:(tkb + 1) * 64],
                    est[:, tkb * 512:(tkb + 1) * 512],
                    start=(tkb == 0), stop=(tkb == TT - 1)))
                fns.append(lambda e, tkb=tkb: e.matmul(
                    psR1[0:1, 0:512], ones_b[:, 0:1],
                    est[:, tkb * 512:(tkb + 1) * 512],
                    start=(tkb == 0), stop=(tkb == TT - 1), skip_group_check=True))
            em.group("tensor", fns)
            em.op("vector", lambda e: e.reciprocal(rr[0:1, :], psR1[0:1, 0:512]))
            em.op("vector", lambda e: e.tensor_copy(rrb[0:1, :], rr[0:1, :]))
            em.op("tensor", lambda e: e.matmul(psBC[0:64, 0:512], ones_r[0:1, 0:64], rrb[0:1, :], start=True, stop=True))
            em.op("vector", lambda e: e.tensor_copy(bcs[:, :], psBC[0:64, 0:512]))
            em.op("vector", lambda e: e.tensor_tensor(ybf[:, :], psYT[0:64, 0:512], bcs[:, :], op=ALU.mult))
            em.dma(at[h * 64:(h + 1) * 64, tqn * 512:(tqn + 1) * 512], ybf[:, :])
    stB.close()

    stC = ExitStack()
    sbC = lambda nm, sh, dt: stC.enter_context(nc.sbuf_tensor(P + nm, sh, dt))
    aT = sbC("aT", [128, DG * N], BF16)
    wpst = sbC("wpst", [128, D], F32)
    wpbf = sbC("wpbf", [128, DG * D], BF16)
    xres = sbC("xres", [128, D], F32)
    s_ = sbC("s_", [128, D], F32)
    sqt = sbC("sqt", [128, D], F32)
    nsum = sbC("nsum", [128, 1], F32)
    ceps = sbC("ceps", [128, 1], F32)
    _ = em.op("vector", lambda e: e.memset(ceps[:, 0:1], EPS))
    wk1 = sbC("wk1", [128, 1], F32)
    o_ = sbC("o_", [128, D], F32)

    for g in range(DG):
        em.dma(aT[:, g * N:(g + 1) * N], at[g * 128:(g + 1) * 128, :])
    for kg in range(DG):
        em.dma(wpst[:], wproj[kg * 128:(kg + 1) * 128, :])
        em.op("vector", lambda e, kg=kg: e.tensor_copy(wpbf[:, kg * D:(kg + 1) * D], wpst[:]))
    for tt in range(TT):
        for nn in range(2):
            pt = ps0 if nn == 0 else ps1
            em.group("tensor", [
                (lambda e, kg=kg, tt=tt, nn=nn, pt=pt: e.matmul(
                    pt[:, 0:512], aT[:, kg * N + tt * 128: kg * N + (tt + 1) * 128],
                    wpbf[:, kg * D + nn * 512: kg * D + (nn + 1) * 512],
                    start=(kg == 0), stop=(kg == DG - 1)))
                for kg in range(DG)])
        em.dma(xres[:], x[tt * 128:(tt + 1) * 128, :])
        em.op("vector", lambda e: e.tensor_tensor(s_[:, 0:512], ps0[:, 0:512], xres[:, 0:512], op=ALU.add))
        em.op("vector", lambda e: e.tensor_tensor(s_[:, 512:1024], ps1[:, 0:512], xres[:, 512:1024], op=ALU.add))
        emit_ln(em, s_, o_, wk1, nsum, sqt, ceps)
        em.dma(x1[tt * 128:(tt + 1) * 128, :], o_[:])
    stC.close()
    st.close()


def emit_l2(nc, em, x1, rw, rb, lw, lb, w1, b1, w2, b2, idn, x2):
    P = "f_"
    st = ExitStack()
    sb = lambda nm, sh, dt: st.enter_context(nc.sbuf_tensor(P + nm, sh, dt))
    ps_ = lambda nm, sh, dt: st.enter_context(nc.psum_tensor(P + nm, sh, dt))

    idn_f = sb("idn_f", [128, 128], F32)
    ones_r = sb("ones_r", [1, 128], BF16)
    x1T = sb("x1T", [128, DG * N], BF16)
    xs = sb("xs", [128, D], F32)
    Wr = sb("Wr", [128, TT * 8], F32)
    lwT = sb("lwT", [128, DG * 4], BF16)
    rwT = sb("rwT", [128, DG * 8], BF16)
    blang = sb("blang", [128, 4], F32)
    brout = sb("brout", [128, 8], F32)
    rowf = sb("rowf", [1, 1024], F32)
    rowb = sb("rowb", [1, 1024], BF16)
    ll = sb("ll", [128, 8], F32)
    lp = sb("lp", [128, 4], F32)
    ew = sb("ew", [128, 4], F32)
    pp = sb("pp", [128, 8], F32)
    ge1 = sb("ge1", [128, 8], F32)
    m1 = sb("m1", [128, 1], F32)
    m2 = sb("m2", [128, 1], F32)
    wk1 = sb("wk1", [128, 1], F32)
    h_bf = sb("h_bf", [128, 32 * 512], BF16)
    b1sb = sb("b1sb", [128, 8 * 32], F32)
    b2bc = sb("b2bc", [128, D], F32)
    w1st = sb("w1st", [128, 2 * DG * 128], F32)
    w1bf = sb("w1bf", [128, 2 * DG * 128], BF16)
    w2st = sb("w2st", [128, 2 * D], F32)
    w2bf = sb("w2bf", [128, 2 * D], BF16)
    tz = sb("tz", [128, 512], F32)
    acc = sb("acc", [128, 4 * D], F32)
    xres = sb("xres", [128, D], F32)
    sqt = sb("sqt", [128, D], F32)
    nsum = sb("nsum", [128, 1], F32)
    ceps = sb("ceps", [128, 1], F32)
    _ = em.op("vector", lambda e: e.memset(ceps[:, 0:1], EPS))
    o_ = sb("o_", [128, D], F32)
    pz = [ps_("pz%d" % i, [128, 512], F32) for i in range(8)]

    em.dma(idn_f[:], idn[:, :])
    em.op("gpsimd", lambda e: e.memset(ones_r[0:1, :], 1.0))
    em.dma(xs[0:4, :], lw[:, :])
    for g in range(DG):
        em.op("tensor", lambda e, g=g: e.transpose(pz[0][:, 0:4], xs[0:4, g * 128:(g + 1) * 128], idn_f[0:4, 0:4]))
        em.op("vector", lambda e, g=g: e.tensor_copy(lwT[:, g * 4:(g + 1) * 4], pz[0][:, 0:4]))
    em.dma(xs[0:8, :], rw[:, :])
    for g in range(DG):
        em.op("tensor", lambda e, g=g: e.transpose(pz[0][:, 0:8], xs[0:8, g * 128:(g + 1) * 128], idn_f[0:8, 0:8]))
        em.op("vector", lambda e, g=g: e.tensor_copy(rwT[:, g * 8:(g + 1) * 8], pz[0][:, 0:8]))
    em.dma(rowf[0:1, 0:4], lb[:].rearrange("(o c) -> o c", o=1))
    em.op("vector", lambda e: e.tensor_copy(rowb[0:1, 0:4], rowf[0:1, 0:4]))
    em.op("tensor", lambda e: e.matmul(pz[0][:, 0:4], ones_r[0:1, 0:128], rowb[0:1, 0:4], start=True, stop=True))
    em.op("vector", lambda e: e.tensor_copy(blang[:], pz[0][:, 0:4]))
    em.dma(rowf[0:1, 0:8], rb[:].rearrange("(o c) -> o c", o=1))
    em.op("vector", lambda e: e.tensor_copy(rowb[0:1, 0:8], rowf[0:1, 0:8]))
    em.op("tensor", lambda e: e.matmul(pz[0][:, 0:8], ones_r[0:1, 0:128], rowb[0:1, 0:8], start=True, stop=True))
    em.op("vector", lambda e: e.tensor_copy(brout[:], pz[0][:, 0:8]))
    for e_ in range(8):
        em.dma_group([
            (b1sb[:, e_ * 32 + m: e_ * 32 + m + 1], b1[e_, m * 128:(m + 1) * 128].rearrange("(p o) -> p o", o=1))
            for m in range(32)])

    for tt in range(TT):
        em.dma(xs[:], x1[tt * 128:(tt + 1) * 128, :])
        for g in range(DG):
            em.op("tensor", lambda e, g=g: e.transpose(pz[1][:, 0:128], xs[:, g * 128:(g + 1) * 128], idn_f[:]))
            em.op("vector", lambda e, g=g, tt=tt: e.tensor_copy(x1T[:, g * N + tt * 128: g * N + tt * 128 + 128], pz[1][:, 0:128]))
        em.group("tensor", [
            (lambda e, g=g, tt=tt: e.matmul(pz[0][:, 0:4], x1T[:, g * N + tt * 128: g * N + (tt) * 128 + 128],
                                            lwT[:, g * 4:(g + 1) * 4], start=(g == 0), stop=(g == DG - 1)))
            for g in range(DG)])
        em.op("vector", lambda e: e.tensor_tensor(lp[:], pz[0][:, 0:4], blang[:], op=ALU.add))
        em.op("vector", lambda e: e.reduce_max(m1[:, 0:1], lp[:], axis=AX.X, negate=True))
        em.op("scalar", lambda e: e.activation(lp[:], lp[:], AF.Exp, bias=m1[:, 0:1], accum_out=nsum[:, 0:1]))
        em.op("vector", lambda e: e.reciprocal(wk1[:, 0:1], nsum[:, 0:1]))
        em.op("vector", lambda e: e.tensor_scalar(lp[:], lp[:], wk1[:, 0:1], None, op0=ALU.mult))
        em.op("vector", lambda e: e.tensor_tensor(ew[:, 0:3], lp[:, 0:3], lp[:, 1:4], op=ALU.add))
        em.op("vector", lambda e: e.tensor_copy(ew[:, 3:4], lp[:, 3:4]))
        em.group("tensor", [
            (lambda e, g=g, tt=tt: e.matmul(pz[0][:, 0:8], x1T[:, g * N + tt * 128: g * N + tt * 128 + 128],
                                            rwT[:, g * 8:(g + 1) * 8], start=(g == 0), stop=(g == DG - 1)))
            for g in range(DG)])
        em.op("vector", lambda e: e.tensor_tensor(ll[:], pz[0][:, 0:8], brout[:], op=ALU.add))
        em.op("vector", lambda e: e.scalar_tensor_tensor(ll[:, 0:4], ew[:], 0.1, ll[:, 0:4], op0=ALU.mult, op1=ALU.add))
        em.op("vector", lambda e: e.scalar_tensor_tensor(ll[:, 4:8], ew[:], 0.1, ll[:, 4:8], op0=ALU.mult, op1=ALU.add))
        em.op("vector", lambda e: e.reduce_max(m1[:, 0:1], ll[:], axis=AX.X, negate=True))
        em.op("scalar", lambda e: e.activation(pp[:], ll[:], AF.Exp, bias=m1[:, 0:1], accum_out=nsum[:, 0:1]))
        em.op("vector", lambda e: e.reciprocal(wk1[:, 0:1], nsum[:, 0:1]))
        em.op("vector", lambda e: e.tensor_scalar(pp[:], pp[:], wk1[:, 0:1], None, op0=ALU.mult))
        em.op("vector", lambda e: e.reduce_max(m1[:, 0:1], pp[:], axis=AX.X))
        em.op("vector", lambda e: e.tensor_scalar(ge1[:], pp[:], m1[:, 0:1], None, op0=ALU.is_ge))
        em.op("vector", lambda e: e.scalar_tensor_tensor(ge1[:], ge1[:], -2.0, pp[:], op0=ALU.mult, op1=ALU.add))
        em.op("vector", lambda e: e.reduce_max(m2[:, 0:1], ge1[:], axis=AX.X))
        em.op("vector", lambda e: e.tensor_tensor(wk1[:, 0:1], m1[:, 0:1], m2[:, 0:1], op=ALU.add))
        em.op("vector", lambda e: e.reciprocal(wk1[:, 0:1], wk1[:, 0:1]))
        em.op("vector", lambda e: e.tensor_scalar(ge1[:], pp[:], m2[:, 0:1], None, op0=ALU.is_ge))
        em.op("vector", lambda e: e.tensor_tensor(ge1[:], pp[:], ge1[:], op=ALU.mult))
        em.op("vector", lambda e, tt=tt: e.tensor_scalar(Wr[:, tt * 8:(tt + 1) * 8], ge1[:], wk1[:, 0:1], None, op0=ALU.mult))

    w1r = [w1[e_].rearrange("(g p) m -> p g m", p=128) for e_ in range(8)]
    for c in range(8):
        for e_ in range(8):
            em.dma(rowf[0:1, :], b2[e_:e_ + 1, :])
            em.op("vector", lambda e: e.tensor_copy(rowb[0:1, :], rowf[0:1, :]))
            em.op("tensor", lambda e: e.matmul(pz[0][:, 0:512], ones_r[0:1, 0:128], rowb[0:1, 0:512], start=True, stop=True))
            em.op("tensor", lambda e: e.matmul(pz[1][:, 0:512], ones_r[0:1, 0:128], rowb[0:1, 512:1024], start=True, stop=True))
            em.op("vector", lambda e: e.tensor_copy(b2bc[:, 0:512], pz[0][:, 0:512]))
            em.op("vector", lambda e: e.tensor_copy(b2bc[:, 512:1024], pz[1][:, 0:512]))
            for s in range(35):
                ents = []
                if s <= 31:
                    b = s % 2
                    ents.append(("sync", lambda e, m=s, b=b, e_=e_: e.dma_start(
                        w1st.rearrange("p (b g m) -> p b g m", b=2, g=DG)[:, b, :, :],
                        w1r[e_][:, :, m * 128:(m + 1) * 128])))
                if 1 <= s <= 32:
                    b = (s - 1) % 2
                    ents.append(("vector", lambda e, b=b: e.tensor_copy(
                        w1bf[:, b * 1024:(b + 1) * 1024], w1st[:, b * 1024:(b + 1) * 1024])))
                if 2 <= s <= 33:
                    m = s - 2
                    b = m % 2
                    pb = 6 + b
                    for kg in range(DG):
                        ents.append(("tensor", lambda e, kg=kg, c=c, b=b, pb=pb: e.matmul(
                            pz[pb][:, 0:512],
                            w1bf[:, b * 1024 + kg * 128: b * 1024 + (kg + 1) * 128],
                            x1T[:, kg * N + c * 512: kg * N + (c + 1) * 512],
                            start=(kg == 0), stop=(kg == DG - 1))))
                if 3 <= s <= 34:
                    m = s - 3
                    pb = 6 + (m % 2)
                    ents.append(("scalar", lambda e, m=m, e_=e_, pb=pb: e.activation(
                        h_bf[:, m * 512:(m + 1) * 512], pz[pb][:, 0:512], AF.Gelu,
                        bias=b1sb[:, e_ * 32 + m: e_ * 32 + m + 1])))
                em.stage(ents)
            for s in range(34):
                ents = []
                if s <= 31:
                    b = s % 2
                    ents.append(("sync", lambda e, kg=s, b=b, e_=e_: e.dma_start(
                        w2st[:, b * D:(b + 1) * D], w2[e_, kg * 128:(kg + 1) * 128, :])))
                if 1 <= s <= 32:
                    b = (s - 1) % 2
                    ents.append(("vector", lambda e, b=b: e.tensor_copy(
                        w2bf[:, b * D:(b + 1) * D], w2st[:, b * D:(b + 1) * D])))
                if 2 <= s <= 33:
                    kg = s - 2
                    b = kg % 2
                    for ts in range(4):
                        for nn in range(2):
                            ents.append(("tensor", lambda e, kg=kg, ts=ts, nn=nn, b=b: e.matmul(
                                pz[ts * 2 + nn][:, 0:512],
                                h_bf[:, kg * 512 + ts * 128: kg * 512 + (ts + 1) * 128],
                                w2bf[:, b * D + nn * 512: b * D + (nn + 1) * 512],
                                start=(kg == 0), stop=(kg == 31))))
                em.stage(ents)
            for ts in range(4):
                for nn in range(2):
                    em.op("vector", lambda e, ts=ts, nn=nn: e.tensor_tensor(
                        tz[:], pz[ts * 2 + nn][:, 0:512], b2bc[:, nn * 512:(nn + 1) * 512], op=ALU.add))
                    if e_ == 0:
                        em.op("vector", lambda e, ts=ts, nn=nn, c=c: e.tensor_scalar(
                            acc[:, ts * D + nn * 512: ts * D + (nn + 1) * 512], tz[:],
                            Wr[:, (c * 4 + ts) * 8 + 0: (c * 4 + ts) * 8 + 1], None, op0=ALU.mult))
                    else:
                        em.op("vector", lambda e, ts=ts, nn=nn, c=c, e_=e_: e.scalar_tensor_tensor(
                            acc[:, ts * D + nn * 512: ts * D + (nn + 1) * 512], tz[:],
                            Wr[:, (c * 4 + ts) * 8 + e_: (c * 4 + ts) * 8 + e_ + 1],
                            acc[:, ts * D + nn * 512: ts * D + (nn + 1) * 512], op0=ALU.mult, op1=ALU.add))
        for ts in range(4):
            tt = c * 4 + ts
            em.dma(xres[:], x1[tt * 128:(tt + 1) * 128, :])
            em.op("vector", lambda e, ts=ts: e.tensor_tensor(xres[:], xres[:], acc[:, ts * D:(ts + 1) * D], op=ALU.add))
            emit_ln(em, xres, o_, wk1, nsum, sqt, ceps)
            em.dma(x2[tt * 128:(tt + 1) * 128, :], o_[:])
    st.close()


def emit_l3(nc, em, x2, nw, nb, ipw, ipb, cw, cb, xpw, xpb, dpw, dpb, alog, dpar, opw, opb, idn, x3):
    P = "m_"
    ipT = nc.dram_tensor(P + "ipT", [D, 2 * DI], BF16, kind="Internal").ap()
    dpT = nc.dram_tensor(P + "dpT", [DI, DI], BF16, kind="Internal").ap()
    opT = nc.dram_tensor(P + "opT", [DI, D], BF16, kind="Internal").ap()

    st = ExitStack()
    sb = lambda nm, sh, dt: st.enter_context(nc.sbuf_tensor(P + nm, sh, dt))
    ps_ = lambda nm, sh, dt: st.enter_context(nc.psum_tensor(P + nm, sh, dt))

    CH = 512
    NCH = N // CH

    idn_f = sb("idn_f", [128, 128], F32)
    ones_r = sb("ones_r", [1, 128], BF16)
    stg = sb("stg", [128, 1024], F32)
    tbf = sb("tbf", [128, 128], BF16)
    nw_bc = sb("nw_bc", [128, D], F32)
    nb_bc = sb("nb_bc", [128, D], F32)
    ob_bc = sb("ob_bc", [128, D], F32)
    rowf = sb("rowf", [1, 1024], F32)
    rowb = sb("rowb", [1, 1024], BF16)
    ipb_sb = sb("ipb_sb", [128, 32], F32)
    cw_sb = sb("cw_sb", [128, DIG * 4], F32)
    cb_sb = sb("cb_sb", [128, DIG], F32)
    xpb_sb = sb("xpb_sb", [32, 1], F32)
    dpb_sb = sb("dpb_sb", [128, DIG], F32)
    an_sb = sb("an_sb", [128, DIG * DS], F32)
    dp_sb = sb("dp_sb", [128, DIG], F32)
    xps_bf = sb("xps_bf", [128, DIG * 32], BF16)
    hstate = sb("hstate", [128, DIG * DS], F32)
    xnT = sb("xnT", [128, DG * CH], BF16)
    xc_bf = sb("xc_bf", [128, DIG * (CH + 3)], BF16)
    z_bf = sb("z_bf", [128, DIG * CH], BF16)
    u_bf = sb("u_bf", [128, DIG * CH], BF16)
    delta = sb("delta", [128, DIG * CH], BF16)
    du_bf = sb("du_bf", [128, DIG * CH], BF16)
    bcB = sb("bcB", [128, DS * CH], BF16)
    bcC = sb("bcC", [128, DS * CH], BF16)
    wstr = sb("wstr", [128, 16 * 128], BF16)
    sp0 = sb("sp0", [128, CH], F32)
    sp1 = sb("sp1", [128, CH], F32)
    sp2 = sb("sp2", [128, CH], F32)
    xdb = sb("xdb", [32, CH], F32)
    rws = sb("rws", [1, CH], F32)
    rwb = sb("rwb", [1, CH], BF16)
    dab = sb("dab", [128, CH], BF16)
    dbu = sb("dbu", [128, CH], BF16)
    hsc = sb("hsc", [128, CH], BF16)
    yg = sb("yg", [128, CH], F32)
    ybf = z_bf
    otst = delta
    ytmp = sb("ytmp", [128, CH], F32)
    zsil = sb("zsil", [128, CH], F32)
    s_tm = sb("s_tm", [128, D], F32)
    xres = sb("xres", [128, D], F32)
    sqt = sb("sqt", [128, D], F32)
    nsum = sb("nsum", [128, 1], F32)
    ceps = sb("ceps", [128, 1], F32)
    _ = em.op("vector", lambda e: e.memset(ceps[:, 0:1], EPS))
    wk1 = sb("wk1", [128, 1], F32)
    o_ = sb("o_", [128, D], F32)
    xn_tm = sb("xn_tm", [128, D], F32)
    cone = sb("cone", [128, 1], F32)
    p0 = ps_("p0", [128, 512], F32)
    p1 = ps_("p1", [128, 512], F32)
    p2 = ps_("p2", [128, 512], F32)
    p3 = ps_("p3", [128, 512], F32)

    em.dma(idn_f[:], idn[:, :])
    em.op("gpsimd", lambda e: e.memset(ones_r[0:1, :], 1.0))
    em.op("vector", lambda e: e.memset(hstate[:], 0.0))
    em.op("vector", lambda e: e.memset(cone[:, 0:1], 1.0))
    em.op("vector", lambda e: e.memset(xc_bf[:], 0.0))

    def prep_transpose(src, rows, cols, dstT):
        for rtile in range(rows // 128):
            for ch in range(0, cols, 1024):
                cw_ = min(1024, cols - ch)
                em.dma(stg[:, 0:cw_], src[rtile * 128:(rtile + 1) * 128, ch:ch + cw_])
                for g in range(cw_ // 128):
                    em.op("tensor", lambda e, g=g: e.transpose(p0[:, 0:128], stg[:, g * 128:(g + 1) * 128], idn_f[:]))
                    em.op("vector", lambda e: e.tensor_copy(tbf[:], p0[:, 0:128]))
                    em.dma(dstT[ch + g * 128:ch + (g + 1) * 128, rtile * 128:(rtile + 1) * 128], tbf[:])

    prep_transpose(ipw, 2 * DI, D, ipT)
    prep_transpose(dpw, DI, DI, dpT)
    prep_transpose(opw, D, DI, opT)
    for ch in range(2):
        em.dma(stg[0:32, 0:1024], xpw[:, ch * 1024:(ch + 1) * 1024])
        for g in range(8):
            em.op("tensor", lambda e, g=g: e.transpose(p0[:, 0:32], stg[0:32, g * 128:(g + 1) * 128], idn_f[0:32, 0:32]))
            em.op("vector", lambda e, g=g, ch=ch: e.tensor_copy(xps_bf[:, (ch * 8 + g) * 32:(ch * 8 + g + 1) * 32], p0[:, 0:32]))

    def bcast_row(dst, src_row1xn, width):
        em.op("vector", lambda e: e.tensor_copy(rowb[0:1, 0:width], src_row1xn))
        nchunks = (width + 511) // 512
        for q in range(nchunks):
            w = min(512, width - q * 512)
            em.op("tensor", lambda e, q=q, w=w: e.matmul(p0[:, 0:w], ones_r[0:1, 0:128], rowb[0:1, q * 512:q * 512 + w], start=True, stop=True))
            em.op("vector", lambda e, q=q, w=w, dst=dst: e.tensor_copy(dst[:, q * 512:q * 512 + w], p0[:, 0:w]))

    em.dma(rowf[0:1, :], nw[:].rearrange("(o c) -> o c", o=1))
    bcast_row(nw_bc, rowf[0:1, 0:D], D)
    em.dma(rowf[0:1, :], nb[:].rearrange("(o c) -> o c", o=1))
    bcast_row(nb_bc, rowf[0:1, 0:D], D)
    em.dma(rowf[0:1, :], opb[:].rearrange("(o c) -> o c", o=1))
    bcast_row(ob_bc, rowf[0:1, 0:D], D)
    em.dma_group([(ipb_sb[:, m:m + 1], ipb[m * 128:(m + 1) * 128].rearrange("(p o) -> p o", o=1)) for m in range(32)])
    em.dma_group([(cw_sb.rearrange("p (g k) -> p g k", k=4)[:, g, :], cw[g * 128:(g + 1) * 128, 0, :]) for g in range(DIG)])
    em.dma_group([(cb_sb[:, g:g + 1], cb[g * 128:(g + 1) * 128].rearrange("(p o) -> p o", o=1)) for g in range(DIG)])
    em.dma(xpb_sb[0:32, 0:1], xpb[:].rearrange("(p o) -> p o", o=1))
    em.dma_group([(dpb_sb[:, g:g + 1], dpb[g * 128:(g + 1) * 128].rearrange("(p o) -> p o", o=1)) for g in range(DIG)])
    em.dma_group([(dp_sb[:, g:g + 1], dpar[g * 128:(g + 1) * 128].rearrange("(p o) -> p o", o=1)) for g in range(DIG)])
    em.dma_group([(an_sb[:, g * DS:(g + 1) * DS], alog[g * 128:(g + 1) * 128, :]) for g in range(DIG)])
    em.op("scalar", lambda e: e.activation(an_sb[:], an_sb[:], AF.Exp))
    em.op("vector", lambda e: e.tensor_scalar(an_sb[:], an_sb[:], -1.0, None, op0=ALU.mult))

    ipT_r = ipT.rearrange("(g p) m -> p g m", p=128)
    dpT_r = dpT.rearrange("(g p) m -> p g m", p=128)
    opT_r = opT.rearrange("(g p) m -> p g m", p=128)

    for c in range(NCH):
        for ts in range(CH // 128):
            tt = c * 4 + ts
            em.dma(xres[:], x2[tt * 128:(tt + 1) * 128, :])
            em.op("vector", lambda e: e.tensor_copy(s_tm[:], xres[:]))
            emit_ln(em, s_tm, xn_tm, wk1, nsum, sqt, ceps)
            em.op("vector", lambda e: e.tensor_tensor(xn_tm[:], xn_tm[:], nw_bc[:], op=ALU.mult))
            em.op("vector", lambda e: e.tensor_tensor(xn_tm[:], xn_tm[:], nb_bc[:], op=ALU.add))
            for g in range(DG):
                em.op("tensor", lambda e, g=g: e.transpose(p0[:, 0:128], xn_tm[:, g * 128:(g + 1) * 128], idn_f[:]))
                em.op("vector", lambda e, g=g, ts=ts: e.tensor_copy(xnT[:, g * CH + ts * 128: g * CH + (ts + 1) * 128], p0[:, 0:128]))
        for m in range(32):
            em.dma(wstr[:, 0:DG * 128].rearrange("p (g m) -> p g m", g=DG), ipT_r[:, :, m * 128:(m + 1) * 128])
            em.group("tensor", [
                (lambda e, kg=kg: e.matmul(p1[:, 0:CH], wstr[:, kg * 128:(kg + 1) * 128],
                                           xnT[:, kg * CH:(kg + 1) * CH], start=(kg == 0), stop=(kg == DG - 1)))
                for kg in range(DG)])
            if m < 16:
                em.op("vector", lambda e, m=m: e.tensor_scalar(
                    xc_bf.rearrange("p (g t) -> p g t", t=CH + 3)[:, m, 3:CH + 3],
                    p1[:, 0:CH], ipb_sb[:, m:m + 1], None, op0=ALU.add))
            else:
                em.op("vector", lambda e, m=m: e.tensor_scalar(
                    z_bf.rearrange("p (g t) -> p g t", t=CH)[:, m - 16, :],
                    p1[:, 0:CH], ipb_sb[:, m:m + 1], None, op0=ALU.add))
        xc_r = xc_bf.rearrange("p (g t) -> p g t", t=CH + 3)
        u_r = u_bf.rearrange("p (g t) -> p g t", t=CH)
        for g in range(DIG):
            cwr = cw_sb.rearrange("p (g k) -> p g k", k=4)
            em.op("vector", lambda e, g=g: e.tensor_scalar(sp1[:], xc_r[:, g, 0:CH], cwr[:, g, 0:1], None, op0=ALU.mult))
            for k in range(1, 4):
                em.op("vector", lambda e, g=g, k=k: e.scalar_tensor_tensor(
                    sp1[:], xc_r[:, g, k:k + CH], cwr[:, g, k:k + 1], sp1[:], op0=ALU.mult, op1=ALU.add))
            em.op("scalar", lambda e, g=g: e.activation(u_r[:, g, :], sp1[:], AF.Silu, bias=cb_sb[:, g:g + 1]))
            em.op("vector", lambda e, g=g: e.tensor_copy(xc_r[:, g, 0:3], xc_r[:, g, CH:CH + 3]))
        em.group("tensor", [
            (lambda e, kg=kg: e.matmul(p1[0:32, 0:CH], xps_bf[:, kg * 32:(kg + 1) * 32],
                                       u_r[:, kg, :], start=(kg == 0), stop=(kg == DIG - 1)))
            for kg in range(DIG)])
        em.op("vector", lambda e: e.tensor_scalar(xdb[0:32, :], p1[0:32, 0:CH], xpb_sb[0:32, 0:1], None, op0=ALU.add))
        for s in range(DS):
            for (src_row, dst) in ((s, bcB), (16 + s, bcC)):
                em.dma(rws[0:1, :], xdb[src_row:src_row + 1, :])
                em.op("vector", lambda e: e.tensor_copy(rwb[0:1, 0:CH], rws[0:1, :]))
                em.op("tensor", lambda e: e.matmul(p2[:, 0:CH], ones_r[0:1, 0:128], rwb[0:1, 0:CH], start=True, stop=True))
                em.op("vector", lambda e, s=s, dst=dst: e.tensor_copy(
                    dst.rearrange("p (s t) -> p s t", t=CH)[:, s, :], p2[:, 0:CH]))
        d_r = delta.rearrange("p (g t) -> p g t", t=CH)
        du_r = du_bf.rearrange("p (g t) -> p g t", t=CH)
        for g in range(DIG):
            em.dma(wstr[:].rearrange("p (g m) -> p g m", g=DIG)[:, 0:8, :], dpT_r[:, 0:8, g * 128:(g + 1) * 128])
            em.dma(wstr[:].rearrange("p (g m) -> p g m", g=DIG)[:, 8:16, :], dpT_r[:, 8:16, g * 128:(g + 1) * 128])
            em.group("tensor", [
                (lambda e, kg=kg: e.matmul(p1[:, 0:CH], wstr[:, kg * 128:(kg + 1) * 128],
                                           u_r[:, kg, :], start=(kg == 0), stop=(kg == DIG - 1)))
                for kg in range(DIG)])
            em.op("vector", lambda e, g=g: e.tensor_scalar(sp0[:], p1[:, 0:CH], dpb_sb[:, g:g + 1], None, op0=ALU.add))
            em.op("vector", lambda e: e.tensor_scalar(sp1[:], sp0[:], 0.0, None, op0=ALU.max))
            em.op("scalar", lambda e: e.activation(sp2[:], sp0[:], AF.Abs))
            em.op("scalar", lambda e: e.activation(sp2[:], sp2[:], AF.Exp, scale=-1.0))
            em.op("scalar", lambda e: e.activation(sp2[:], sp2[:], AF.Ln, bias=cone[:, 0:1]))
            em.op("vector", lambda e, g=g: e.tensor_tensor(d_r[:, g, :], sp1[:], sp2[:], op=ALU.add))
            em.op("vector", lambda e, g=g: e.tensor_tensor(du_r[:, g, :], d_r[:, g, :], u_r[:, g, :], op=ALU.mult))
        bcB_r = bcB.rearrange("p (s t) -> p s t", t=CH)
        bcC_r = bcC.rearrange("p (s t) -> p s t", t=CH)
        y_r = ybf.rearrange("p (g t) -> p g t", t=CH)
        for g in range(DIG):
            for s in range(DS):
                em.op("scalar", lambda e, g=g, s=s: e.activation(dab[:], d_r[:, g, :], AF.Exp, scale=an_sb[:, g * DS + s: g * DS + s + 1]))
                em.op("vector", lambda e, g=g, s=s: e.tensor_tensor(dbu[:], du_r[:, g, :], bcB_r[:, s, :], op=ALU.mult))
                em.op("vector", lambda e, g=g, s=s: e.tensor_tensor_scan(
                    hsc[:], dab[:], dbu[:], hstate[:, g * DS + s: g * DS + s + 1], op0=ALU.mult, op1=ALU.add))
                em.op("vector", lambda e, g=g, s=s: e.tensor_copy(hstate[:, g * DS + s: g * DS + s + 1], hsc[:, CH - 1:CH]))
                if s == 0:
                    em.op("vector", lambda e, g=g, s=s: e.tensor_tensor(yg[:], hsc[:], bcC_r[:, s, :], op=ALU.mult))
                else:
                    em.op("vector", lambda e, g=g, s=s: e.tensor_tensor(ytmp[:], hsc[:], bcC_r[:, s, :], op=ALU.mult))
                    em.op("vector", lambda e: e.tensor_tensor(yg[:], yg[:], ytmp[:], op=ALU.add))
            em.op("vector", lambda e, g=g: e.scalar_tensor_tensor(yg[:], u_r[:, g, :], dp_sb[:, g:g + 1], yg[:], op0=ALU.mult, op1=ALU.add))
            em.op("scalar", lambda e, g=g: e.activation(zsil[:], z_bf.rearrange("p (g t) -> p g t", t=CH)[:, g, :], AF.Silu))
            em.op("vector", lambda e, g=g: e.tensor_tensor(y_r[:, g, :], yg[:], zsil[:], op=ALU.mult))
        du_st = du_bf.rearrange("p (a b) -> p a b", a=16)
        for nn in range(2):
            for q in range(4):
                em.dma(otst[:].rearrange("p (g m) -> p g m", g=DIG)[:, q * 4:(q + 1) * 4, :],
                       opT_r[:, q * 4:(q + 1) * 4, nn * 512:(nn + 1) * 512])
            for ts in range(4):
                em.group("tensor", [
                    (lambda e, kg=kg, ts=ts: e.matmul(p3[:, 0:512],
                                                      y_r[:, kg, ts * 128:(ts + 1) * 128],
                                                      otst[:, kg * 512:(kg + 1) * 512],
                                                      start=(kg == 0), stop=(kg == DIG - 1)))
                    for kg in range(DIG)])
                em.op("vector", lambda e, ts=ts, nn=nn: e.tensor_tensor(
                    du_st[:, ts * 2 + nn, 0:512], p3[:, 0:512],
                    ob_bc[:, nn * 512:(nn + 1) * 512], op=ALU.add))
        for ts in range(4):
            tt = c * 4 + ts
            em.dma(xres[:], x2[tt * 128:(tt + 1) * 128, :])
            for nn in range(2):
                em.op("vector", lambda e, ts=ts, nn=nn: e.scalar_tensor_tensor(
                    s_tm[:, nn * 512:(nn + 1) * 512], xres[:, nn * 512:(nn + 1) * 512], 2.0,
                    du_st[:, ts * 2 + nn, 0:512], op0=ALU.mult, op1=ALU.add))
            emit_ln(em, s_tm, o_, wk1, nsum, sqt, ceps)
            em.dma(x3[tt * 128:(tt + 1) * 128, :], o_[:])
    st.close()


def emit_l4(nc, em, x3, rW, rb, sW1, sb1, sW2, sb2, eW1, eb1, eW2, eb2, idn, out):
    P = "g_"
    E = 15
    H = 1024
    st = ExitStack()
    sb_ = lambda nm, sh, dt: st.enter_context(nc.sbuf_tensor(P + nm, sh, dt))
    ps_ = lambda nm, sh, dt: st.enter_context(nc.psum_tensor(P + nm, sh, dt))

    idn_f = sb_("idn_f", [128, 128], F32)
    ones_r = sb_("ones_r", [1, 128], BF16)
    xs = sb_("xs", [128, D], F32)
    x3T = sb_("x3T", [128, DG * 512], BF16)
    rwbf = sb_("rwbf", [128, DG * E], BF16)
    brb = sb_("brb", [128, E], F32)
    sw1bf = sb_("sw1bf", [128, DG * H], BF16)
    sw2bf = sb_("sw2bf", [128, DG * D], BF16)
    sb1p = sb_("sb1p", [128, 8], F32)
    sb2bc = sb_("sb2bc", [128, D], F32)
    eb1p = sb_("eb1p", [128, E * 8], F32)
    eb2bc = sb_("eb2bc", [128, D], F32)
    rowf = sb_("rowf", [1, 1024], F32)
    rowb = sb_("rowb", [1, 1024], BF16)
    wst = sb_("wst", [128, 2048], F32)
    wbf = sb_("wbf", [128, 2048], BF16)
    hsh = sb_("hsh", [128, 8 * 512], BF16)
    he = sb_("he", [128, 8 * 512], BF16)
    Wr = sb_("Wr", [128, 4 * 16], F32)
    pp = sb_("pp", [128, E], F32)
    ge1 = sb_("ge1", [128, E], F32)
    m1 = sb_("m1", [128, 1], F32)
    m2 = sb_("m2", [128, 1], F32)
    acc = sb_("acc", [128, 4 * D], F32)
    tz = sb_("tz", [128, 512], F32)
    xres = sb_("xres", [128, D], F32)
    sqt = sb_("sqt", [128, D], F32)
    nsum = sb_("nsum", [128, 1], F32)
    ceps = sb_("ceps", [128, 1], F32)
    _ = em.op("vector", lambda e: e.memset(ceps[:, 0:1], EPS))
    wk1 = sb_("wk1", [128, 1], F32)
    o_ = sb_("o_", [128, D], F32)
    pz = [ps_("pz%d" % i, [128, 512], F32) for i in range(8)]

    em.dma(idn_f[:], idn[:, :])
    em.op("gpsimd", lambda e: e.memset(ones_r[0:1, :], 1.0))

    def bcast_row(dst, width):
        em.op("vector", lambda e: e.tensor_copy(rowb[0:1, 0:width], rowf[0:1, 0:width]))
        for q in range((width + 511) // 512):
            w = min(512, width - q * 512)
            em.op("tensor", lambda e, q=q, w=w: e.matmul(pz[6][:, 0:w], ones_r[0:1, 0:128], rowb[0:1, q * 512:q * 512 + w], start=True, stop=True))
            em.op("vector", lambda e, q=q, w=w, dst=dst: e.tensor_copy(dst[:, q * 512:q * 512 + w], pz[6][:, 0:w]))

    for g in range(DG):
        em.dma(wst[:, 0:E], rW[g * 128:(g + 1) * 128, :])
        em.op("vector", lambda e, g=g: e.tensor_copy(rwbf[:, g * E:(g + 1) * E], wst[:, 0:E]))
    em.dma(rowf[0:1, 0:E], rb[:].rearrange("(o c) -> o c", o=1))
    bcast_row(brb, E)
    em.dma(rowf[0:1, :], sb2[:].rearrange("(o c) -> o c", o=1))
    bcast_row(sb2bc, D)
    em.dma_group([(sb1p[:, m:m + 1], sb1[m * 128:(m + 1) * 128].rearrange("(p o) -> p o", o=1)) for m in range(8)])
    for e_ in range(E):
        em.dma_group([(eb1p[:, e_ * 8 + m: e_ * 8 + m + 1], eb1[e_, m * 128:(m + 1) * 128].rearrange("(p o) -> p o", o=1)) for m in range(8)])
    for kg in range(DG):
        em.dma(wst[:, 0:H], sW1[kg * 128:(kg + 1) * 128, :])
        em.op("vector", lambda e, kg=kg: e.tensor_copy(sw1bf[:, kg * H:(kg + 1) * H], wst[:, 0:H]))
        em.dma(wst[:, 0:D], sW2[kg * 128:(kg + 1) * 128, :])
        em.op("vector", lambda e, kg=kg: e.tensor_copy(sw2bf[:, kg * D:(kg + 1) * D], wst[:, 0:D]))

    for c in range(8):
        for ts in range(4):
            tt = c * 4 + ts
            em.dma(xs[:], x3[tt * 128:(tt + 1) * 128, :])
            for g in range(DG):
                em.op("tensor", lambda e, g=g: e.transpose(pz[7][:, 0:128], xs[:, g * 128:(g + 1) * 128], idn_f[:]))
                em.op("vector", lambda e, g=g, ts=ts: e.tensor_copy(x3T[:, g * 512 + ts * 128: g * 512 + (ts + 1) * 128], pz[7][:, 0:128]))
            em.group("tensor", [
                (lambda e, g=g, ts=ts: e.matmul(pz[0][:, 0:E], x3T[:, g * 512 + ts * 128: g * 512 + ts * 128 + 128],
                                                rwbf[:, g * E:(g + 1) * E], start=(g == 0), stop=(g == DG - 1)))
                for g in range(DG)])
            em.op("vector", lambda e: e.tensor_tensor(pp[:], pz[0][:, 0:E], brb[:], op=ALU.add))
            em.op("vector", lambda e: e.reduce_max(m1[:, 0:1], pp[:], axis=AX.X, negate=True))
            em.op("scalar", lambda e: e.activation(pp[:], pp[:], AF.Exp, bias=m1[:, 0:1], accum_out=nsum[:, 0:1]))
            em.op("vector", lambda e: e.reciprocal(wk1[:, 0:1], nsum[:, 0:1]))
            em.op("vector", lambda e: e.tensor_scalar(pp[:], pp[:], wk1[:, 0:1], None, op0=ALU.mult))
            em.op("vector", lambda e: e.reduce_max(m1[:, 0:1], pp[:], axis=AX.X))
            em.op("vector", lambda e: e.tensor_scalar(ge1[:], pp[:], m1[:, 0:1], None, op0=ALU.is_ge))
            em.op("vector", lambda e: e.scalar_tensor_tensor(ge1[:], ge1[:], -2.0, pp[:], op0=ALU.mult, op1=ALU.add))
            em.op("vector", lambda e: e.reduce_max(m2[:, 0:1], ge1[:], axis=AX.X))
            em.op("vector", lambda e: e.tensor_scalar(sqt[:, 0:E], ge1[:], m2[:, 0:1], None, op0=ALU.is_ge))
            em.op("vector", lambda e: e.scalar_tensor_tensor(ge1[:], sqt[:, 0:E], -2.0, ge1[:], op0=ALU.mult, op1=ALU.add))
            em.op("vector", lambda e: e.reduce_max(m2[:, 0:1], ge1[:], axis=AX.X))
            em.op("vector", lambda e: e.tensor_scalar(ge1[:], pp[:], m2[:, 0:1], None, op0=ALU.is_ge))
            em.op("vector", lambda e, ts=ts: e.tensor_tensor(Wr[:, ts * 16: ts * 16 + E], pp[:], ge1[:], op=ALU.mult))
        for m in range(8):
            em.group("tensor", [
                (lambda e, kg=kg, m=m: e.matmul(pz[0][:, 0:512], sw1bf[:, kg * H + m * 128: kg * H + (m + 1) * 128],
                                                x3T[:, kg * 512:(kg + 1) * 512], start=(kg == 0), stop=(kg == DG - 1)))
                for kg in range(DG)])
            em.op("scalar", lambda e, m=m: e.activation(hsh[:, m * 512:(m + 1) * 512], pz[0][:, 0:512], AF.Relu, bias=sb1p[:, m:m + 1]))
        fns = []
        for kg in range(8):
            for ts in range(4):
                for nn in range(2):
                    fns.append(lambda e, kg=kg, ts=ts, nn=nn: e.matmul(
                        pz[ts * 2 + nn][:, 0:512],
                        hsh[:, kg * 512 + ts * 128: kg * 512 + (ts + 1) * 128],
                        sw2bf[:, kg * D + nn * 512: kg * D + (nn + 1) * 512],
                        start=(kg == 0), stop=(kg == 7)))
        em.group("tensor", fns)
        for ts in range(4):
            for nn in range(2):
                em.op("vector", lambda e, ts=ts, nn=nn: e.tensor_tensor(
                    acc[:, ts * D + nn * 512: ts * D + (nn + 1) * 512],
                    pz[ts * 2 + nn][:, 0:512], sb2bc[:, nn * 512:(nn + 1) * 512], op=ALU.add))
        for e_ in range(E):
            em.dma(rowf[0:1, :], eb2[e_:e_ + 1, :])
            bcast_row(eb2bc, D)
            for s in range(11):
                ents = []
                if s <= 7:
                    b = s % 2
                    ents.append(("sync", lambda e, m=s, b=b, e_=e_: e.dma_start(
                        wst.rearrange("p (b g m) -> p b g m", b=2, g=DG)[:, b, :, :],
                        eW1[e_].rearrange("(g p) m -> p g m", p=128)[:, :, m * 128:(m + 1) * 128])))
                if 1 <= s <= 8:
                    b = (s - 1) % 2
                    ents.append(("vector", lambda e, b=b: e.tensor_copy(
                        wbf[:, b * 1024:(b + 1) * 1024], wst[:, b * 1024:(b + 1) * 1024])))
                if 2 <= s <= 9:
                    m = s - 2
                    b = m % 2
                    pb = 6 + b
                    for kg in range(DG):
                        ents.append(("tensor", lambda e, kg=kg, b=b, pb=pb: e.matmul(
                            pz[pb][:, 0:512],
                            wbf[:, b * 1024 + kg * 128: b * 1024 + (kg + 1) * 128],
                            x3T[:, kg * 512:(kg + 1) * 512],
                            start=(kg == 0), stop=(kg == DG - 1))))
                if 3 <= s <= 10:
                    m = s - 3
                    pb = 6 + (m % 2)
                    ents.append(("scalar", lambda e, m=m, e_=e_, pb=pb: e.activation(
                        he[:, m * 512:(m + 1) * 512], pz[pb][:, 0:512], AF.Relu,
                        bias=eb1p[:, e_ * 8 + m: e_ * 8 + m + 1])))
                em.stage(ents)
            for s in range(10):
                ents = []
                if s <= 7:
                    b = s % 2
                    ents.append(("sync", lambda e, kg=s, b=b, e_=e_: e.dma_start(
                        wst[:, b * 1024:(b + 1) * 1024], eW2[e_, kg * 128:(kg + 1) * 128, :])))
                if 1 <= s <= 8:
                    b = (s - 1) % 2
                    ents.append(("vector", lambda e, b=b: e.tensor_copy(
                        wbf[:, b * 1024:(b + 1) * 1024], wst[:, b * 1024:(b + 1) * 1024])))
                if 2 <= s <= 9:
                    kg = s - 2
                    b = kg % 2
                    for ts in range(4):
                        for nn in range(2):
                            ents.append(("tensor", lambda e, kg=kg, ts=ts, nn=nn, b=b: e.matmul(
                                pz[ts * 2 + nn][:, 0:512],
                                he[:, kg * 512 + ts * 128: kg * 512 + (ts + 1) * 128],
                                wbf[:, b * 1024 + nn * 512: b * 1024 + (nn + 1) * 512],
                                start=(kg == 0), stop=(kg == 7))))
                em.stage(ents)
            for ts in range(4):
                for nn in range(2):
                    em.op("vector", lambda e, ts=ts, nn=nn: e.tensor_tensor(
                        tz[:], pz[ts * 2 + nn][:, 0:512], eb2bc[:, nn * 512:(nn + 1) * 512], op=ALU.add))
                    em.op("vector", lambda e, ts=ts, nn=nn, e_=e_: e.scalar_tensor_tensor(
                        acc[:, ts * D + nn * 512: ts * D + (nn + 1) * 512], tz[:],
                        Wr[:, ts * 16 + e_: ts * 16 + e_ + 1],
                        acc[:, ts * D + nn * 512: ts * D + (nn + 1) * 512], op0=ALU.mult, op1=ALU.add))
        for ts in range(4):
            tt = c * 4 + ts
            em.dma(xres[:], x3[tt * 128:(tt + 1) * 128, :])
            em.op("vector", lambda e, ts=ts: e.tensor_tensor(xres[:], xres[:], acc[:, ts * D:(ts + 1) * D], op=ALU.add))
            emit_ln(em, xres, o_, wk1, nsum, sqt, ceps)
            em.dma(out[tt * 128:(tt + 1) * 128, :], o_[:])
    st.close()


def build_fused():
    nc = bass.Bass("TRN2", target_bir_lowering=False)
    d = lambda nm, sh: nc.dram_tensor(nm, sh, F32, kind="ExternalInput").ap()
    x = d("x", [N, D])
    wqkv = d("wqkv", [D, 3 * D])
    wproj = d("wproj", [D, D])
    cosq = d("cosq", [128, N])
    sinq = d("sinq", [128, N])
    cosk = d("cosk", [128, N])
    sink = d("sink", [128, N])
    rt = d("rt", [128, 128])
    idn = d("idn", [128, 128])
    f_rw = d("f_rw", [8, D])
    f_rb = d("f_rb", [8])
    f_lw = d("f_lw", [4, D])
    f_lb = d("f_lb", [4])
    f_w1 = d("f_w1", [8, D, 4096])
    f_b1 = d("f_b1", [8, 4096])
    f_w2 = d("f_w2", [8, 4096, D])
    f_b2 = d("f_b2", [8, D])
    m_nw = d("m_nw", [D])
    m_nb = d("m_nb", [D])
    m_ipw = d("m_ipw", [2 * DI, D])
    m_ipb = d("m_ipb", [2 * DI])
    m_cw = d("m_cw", [DI, 1, 4])
    m_cb = d("m_cb", [DI])
    m_xpw = d("m_xpw", [2 * DS, DI])
    m_xpb = d("m_xpb", [2 * DS])
    m_dpw = d("m_dpw", [DI, DI])
    m_dpb = d("m_dpb", [DI])
    m_alog = d("m_alog", [DI, DS])
    m_dpar = d("m_dpar", [DI])
    m_opw = d("m_opw", [D, DI])
    m_opb = d("m_opb", [D])
    g_rW = d("g_rW", [D, 15])
    g_rb = d("g_rb", [15])
    g_sW1 = d("g_sW1", [D, 1024])
    g_sb1 = d("g_sb1", [1024])
    g_sW2 = d("g_sW2", [1024, D])
    g_sb2 = d("g_sb2", [D])
    g_eW1 = d("g_eW1", [15, D, 1024])
    g_eb1 = d("g_eb1", [15, 1024])
    g_eW2 = d("g_eW2", [15, 1024, D])
    g_eb2 = d("g_eb2", [15, D])
    out = nc.dram_tensor("out", [N, D], F32, kind="ExternalOutput").ap()
    x1 = nc.dram_tensor("x1", [N, D], F32, kind="Internal").ap()
    x2 = nc.dram_tensor("x2", [N, D], F32, kind="Internal").ap()
    x3 = nc.dram_tensor("x3", [N, D], F32, kind="Internal").ap()

    em = Em()
    emit_l1(nc, em, x, wqkv, wproj, cosq, sinq, cosk, sink, rt, idn, x1)
    emit_l2(nc, em, x1, f_rw, f_rb, f_lw, f_lb, f_w1, f_b1, f_w2, f_b2, idn, x2)
    emit_l3(nc, em, x2, m_nw, m_nb, m_ipw, m_ipb, m_cw, m_cb, m_xpw, m_xpb, m_dpw, m_dpb,
            m_alog, m_dpar, m_opw, m_opb, idn, x3)
    emit_l4(nc, em, x3, g_rW, g_rb, g_sW1, g_sb1, g_sW2, g_sb2, g_eW1, g_eb1, g_eW2, g_eb2, idn, out)
    with nc.semaphore("tok") as tok:
        em.build(nc, tok)
    return nc


_CACHE = {}


def _get(name, builder):
    if name not in _CACHE:
        _CACHE[name] = builder()
    return _CACHE[name]


def kernel(**inp):
    cosq, sinq, cosk, sink, rt, idn = host_consts()
    cores = list(range(8))
    x_full = np.ascontiguousarray(inp["L0_attn_x"].astype(np.float32))
    nc = _get("fused", build_fused)
    maps = [dict(
        x=np.ascontiguousarray(x_full[c]), wqkv=inp["L0_attn_w_qkv"], wproj=inp["L0_attn_w_proj"],
        cosq=cosq, sinq=sinq, cosk=cosk, sink=sink, rt=rt, idn=idn,
        f_rw=inp["L0_ffn_router_w"], f_rb=inp["L0_ffn_router_b"], f_lw=inp["L0_ffn_lang_w"],
        f_lb=inp["L0_ffn_lang_b"], f_w1=inp["L0_ffn_w1"], f_b1=inp["L0_ffn_b1"],
        f_w2=inp["L0_ffn_w2"], f_b2=inp["L0_ffn_b2"],
        m_nw=inp["L1_attn_norm_w"], m_nb=inp["L1_attn_norm_b"], m_ipw=inp["L1_attn_in_proj_w"],
        m_ipb=inp["L1_attn_in_proj_b"], m_cw=inp["L1_attn_conv_w"], m_cb=inp["L1_attn_conv_b"],
        m_xpw=inp["L1_attn_x_proj_w"], m_xpb=inp["L1_attn_x_proj_b"], m_dpw=inp["L1_attn_dt_proj_w"],
        m_dpb=inp["L1_attn_dt_proj_b"], m_alog=inp["L1_attn_A_log"], m_dpar=inp["L1_attn_D_param"],
        m_opw=inp["L1_attn_out_proj_w"], m_opb=inp["L1_attn_out_proj_b"],
        g_rW=inp["L1_ffn_router_W"], g_rb=inp["L1_ffn_router_b"], g_sW1=inp["L1_ffn_sW1"],
        g_sb1=inp["L1_ffn_sb1"], g_sW2=inp["L1_ffn_sW2"], g_sb2=inp["L1_ffn_sb2"],
        g_eW1=inp["L1_ffn_eW1"], g_eb1=inp["L1_ffn_eb1"], g_eW2=inp["L1_ffn_eW2"],
        g_eb2=inp["L1_ffn_eb2"]) for c in cores]
    r = run_bass_kernel_spmd(nc, maps, core_ids=cores)
    outs = [r.results[c]["out"] for c in cores]
    return np.stack(outs, axis=0).astype(np.float32)


# version 07 (final)
# speedup vs baseline: 1.4729x; 1.2215x over previous
import numpy as np
import ml_dtypes
import concourse.bass as bass
import concourse.mybir as mybir
from concourse.bass_utils import run_bass_kernel_spmd
from contextlib import ExitStack

F32 = mybir.dt.float32
BF16 = mybir.dt.bfloat16
AF = mybir.ActivationFunctionType
ALU = mybir.AluOpType
AX = mybir.AxisListType

ENGINES = ("tensor", "vector", "scalar", "sync", "gpsimd")

N = 4096
D = 1024
TT = N // 128
DG = D // 128
HEADS = 16
HD = 64
DI = 2048
DIG = DI // 128
DS = 16
EPS = 1e-5


class Em:
    def __init__(self):
        self.items = []
        self.val = 0

    def op(self, eng, fn, inc=1):
        self.items.append((eng, self.val, inc, fn))
        self.val += inc

    def group(self, eng, fns, inc=1):
        w = self.val
        for i, fn in enumerate(fns):
            self.items.append((eng, w if i == 0 else None, inc, fn))
            self.val += inc

    def stage(self, entries):
        w = self.val
        for ent in entries:
            eng, fn = ent[0], ent[1]
            inc = ent[2] if len(ent) > 2 else (16 if eng == "sync" else 1)
            self.items.append((eng, w, inc, fn))
            self.val += inc
        return self.val

    def dma(self, dst, src, eng="sync"):
        self.op(eng, lambda e, d=dst, s=src: e.dma_start(d, s), inc=16)

    def dma_group(self, pairs, eng="sync"):
        self.group(eng, [lambda e, d=d_, s=s_: e.dma_start(d, s) for (d_, s_) in pairs], inc=16)

    def build(self, nc, tok):
        final = self.val
        with nc.Block() as block:
            for eng_name in ENGINES:
                ops = [it for it in self.items if it[0] == eng_name]
                if not ops and eng_name != "sync":
                    continue

                def section(engine, ops=ops, eng_name=eng_name):
                    for (_, wait, inc, fn) in ops:
                        if wait is not None and wait > 0:
                            engine.wait_ge(tok, wait)
                        inst = fn(engine)
                        if inc:
                            inst.then_inc(tok, inc)
                    if eng_name == "sync":
                        engine.wait_ge(tok, final)

                getattr(block, eng_name)(section)


def emit_ln(em, s, out, work, nsum, sqt, ceps):
    em.op("vector", lambda e: e.tensor_reduce(nsum[:, 0:1], s[:], axis=AX.X, op=ALU.add, negate=True))
    em.op("scalar", lambda e: e.activation(work[:, 0:1], nsum[:, 0:1], AF.Copy, scale=1.0 / D))
    em.op("vector", lambda e: e.tensor_scalar(s[:], s[:], work[:, 0:1], None, op0=ALU.add))
    em.op("scalar", lambda e: e.activation(sqt[:], s[:], AF.Square, accum_out=nsum[:, 0:1]))
    em.op("scalar", lambda e: e.activation(work[:, 0:1], nsum[:, 0:1], AF.Ln, scale=1.0 / D, bias=ceps[:, 0:1]))
    em.op("scalar", lambda e: e.activation(work[:, 0:1], work[:, 0:1], AF.Exp, scale=-0.5))
    em.op("vector", lambda e: e.tensor_scalar(out[:], s[:], work[:, 0:1], None, op0=ALU.mult))


def host_consts():
    inv = 1.0 / (10000.0 ** (np.arange(0, HD, 2, dtype=np.float64) / HD))
    freqs = np.arange(N, dtype=np.float64)[:, None] * inv[None, :]
    cosv = np.cos(freqs)
    sinv = np.sin(freqs)
    fidx = (np.arange(128) % 64) // 2
    cosk = cosv[:, fidx].T.astype(np.float32).copy()
    sink = sinv[:, fidx].T.astype(np.float32).copy()
    cosq = (0.125 * cosk).astype(np.float32)
    sinq = (0.125 * sink).astype(np.float32)
    rt = np.zeros((128, 128), np.float32)
    for b in (0, 64):
        for i in range(32):
            rt[b + 2 * i + 1, b + 2 * i] = -1.0
            rt[b + 2 * i, b + 2 * i + 1] = 1.0
    idn = np.eye(128, dtype=np.float32)
    return cosq, sinq, cosk, sink, rt, idn


def emit_l1(nc, em, x, wqkv, wproj, cosq, sinq, cosk, sink, rt, idn, x1):
    P = "a_"
    qf = nc.dram_tensor(P + "qf", [D, N], BF16, kind="Internal").ap()
    kf = nc.dram_tensor(P + "kf", [D, N], BF16, kind="Internal").ap()
    vf = nc.dram_tensor(P + "vf", [N, D], BF16, kind="Internal").ap()
    at = nc.dram_tensor(P + "at", [D, N], BF16, kind="Internal").ap()

    st = ExitStack()
    sb = lambda nm, sh, dt: st.enter_context(nc.sbuf_tensor(P + nm, sh, dt))
    ps = lambda nm, sh, dt: st.enter_context(nc.psum_tensor(P + nm, sh, dt))

    idn_f = sb("idn_f", [128, 128], F32)
    idn_b = sb("idn_b", [128, 128], BF16)
    rt_bf = sb("rt_bf", [128, 128], BF16)
    stage1 = sb("stage1", [128, 128], F32)
    xs = sb("xs", [128, D], F32)
    ones_b = sb("ones_b", [128, 1], BF16)
    ones_r = sb("ones_r", [1, 128], BF16)
    ps0 = ps("ps0", [128, 512], F32)
    ps1 = ps("ps1", [128, 512], F32)

    em.dma_group([(idn_f[:], idn[:, :]), (stage1[:], rt[:, :])])
    em.op("vector", lambda e: e.tensor_copy(idn_b[:], idn_f[:]))
    em.op("vector", lambda e: e.tensor_copy(rt_bf[:], stage1[:]))
    em.op("gpsimd", lambda e: e.memset(ones_b[:, 0:1], 1.0))
    em.op("gpsimd", lambda e: e.memset(ones_r[0:1, :], 1.0))

    stA = ExitStack()
    sbA = lambda nm, sh, dt: stA.enter_context(nc.sbuf_tensor(P + nm, sh, dt))
    xT = sbA("xT", [128, DG * N], BF16)
    cq = sbA("cq", [128, N], F32)
    sq_ = sbA("sq_", [128, N], F32)
    ck = sbA("ck", [128, N], F32)
    sk = sbA("sk", [128, N], F32)
    wst = sbA("wst", [128, DG * 128], F32)
    wbf = sbA("wbf", [128, DG * 128], BF16)
    qkb = sbA("qkb", [128, 512], BF16)
    t1 = sbA("t1", [128, 512], F32)
    t2 = sbA("t2", [128, 512], F32)
    obf = sbA("obf", [128, 512], BF16)
    vbt = sbA("vbt", [128, 128], BF16)

    em.dma_group([(cq[:], cosq[:, :]), (sq_[:], sinq[:, :]), (ck[:], cosk[:, :]), (sk[:], sink[:, :])])

    for tt in range(TT):
        em.dma(xs[:], x[tt * 128:(tt + 1) * 128, :])
        for g in range(DG):
            em.op("tensor", lambda e, g=g: e.transpose(ps0[:, 0:128], xs[:, g * 128:(g + 1) * 128], idn_f[:]))
            em.op("vector", lambda e, g=g, tt=tt: e.tensor_copy(xT[:, g * N + tt * 128: g * N + tt * 128 + 128], ps0[:, 0:128]))

    wqkv_r = wqkv.rearrange("(g p) m -> p g m", p=128)
    for m in range(16):
        em.dma(wst.rearrange("p (g m) -> p g m", g=DG)[:, :, :], wqkv_r[:, :, m * 128:(m + 1) * 128])
        em.op("vector", lambda e: e.tensor_copy(wbf[:], wst[:]))
        is_q = m < 8
        ctab = cq if is_q else ck
        stab = sq_ if is_q else sk
        dst = qf if is_q else kf
        row = (m if is_q else m - 8) * 128
        for n in range(8):
            em.group("tensor", [
                (lambda e, kg=kg, n=n: e.matmul(ps0[:, 0:512], wbf[:, kg * 128:(kg + 1) * 128],
                                                xT[:, kg * N + n * 512: kg * N + (n + 1) * 512],
                                                start=(kg == 0), stop=(kg == DG - 1)))
                for kg in range(DG)])
            em.op("vector", lambda e: e.tensor_copy(qkb[:], ps0[:, 0:512]))
            em.op("tensor", lambda e: e.matmul(ps1[:, 0:512], rt_bf[:], qkb[:], start=True, stop=True))
            em.op("vector", lambda e, n=n, ctab=ctab: e.tensor_tensor(t1[:], ps0[:, 0:512], ctab[:, n * 512:(n + 1) * 512], op=ALU.mult))
            em.op("vector", lambda e, n=n, stab=stab: e.tensor_tensor(t2[:], ps1[:, 0:512], stab[:, n * 512:(n + 1) * 512], op=ALU.mult))
            em.op("vector", lambda e: e.tensor_tensor(obf[:], t1[:], t2[:], op=ALU.add))
            em.dma(dst[row:row + 128, n * 512:(n + 1) * 512], obf[:])
    for m in range(8):
        em.dma(wst.rearrange("p (g m) -> p g m", g=DG)[:, :, :], wqkv_r[:, :, (16 + m) * 128:(17 + m) * 128])
        em.op("vector", lambda e: e.tensor_copy(wbf[:], wst[:]))
        for tt in range(TT):
            em.group("tensor", [
                (lambda e, kg=kg, tt=tt: e.matmul(ps0[:, 0:128],
                                                  xT[:, kg * N + tt * 128: kg * N + (tt + 1) * 128],
                                                  wbf[:, kg * 128:(kg + 1) * 128],
                                                  start=(kg == 0), stop=(kg == DG - 1)))
                for kg in range(DG)])
            em.op("vector", lambda e: e.tensor_copy(vbt[:], ps0[:, 0:128]))
            em.dma(vf[tt * 128:(tt + 1) * 128, m * 128:(m + 1) * 128], vbt[:])
    stA.close()

    stB = ExitStack()
    sbB = lambda nm, sh, dt: stB.enter_context(nc.sbuf_tensor(P + nm, sh, dt))
    qh = sbB("qh", [64, N], BF16)
    kh = sbB("kh", [64, N], BF16)
    vh = sbB("vh", [128, TT * 64], BF16)
    est = sbB("est", [128, TT * 512], BF16)
    rr = sbB("rr", [1, 512], F32)
    rrb = sbB("rrb", [1, 512], BF16)
    bcs = sbB("bcs", [64, 512], F32)
    ybf = sbB("ybf", [64, 512], BF16)
    psST = ps("psST", [128, 512], F32)
    psST2 = ps("psST2", [128, 512], F32)
    psYT = ps("psYT", [128, 512], F32)
    psR1 = ps("psR1", [128, 512], F32)
    psBC = ps("psBC", [128, 512], F32)

    vf_r = vf.rearrange("(tt p) c -> p tt c", p=128)
    for h in range(HEADS):
        em.dma(qh[:, :], qf[h * 64:(h + 1) * 64, :])
        em.dma(kh[:, :], kf[h * 64:(h + 1) * 64, :])
        em.dma_group([
            (vh.rearrange("p (tt c) -> p tt c", c=64)[:, q * 8:(q + 1) * 8, :],
             vf_r[:, q * 8:(q + 1) * 8, h * 64:(h + 1) * 64])
            for q in range(4)])
        for tqn in range(8):
            psS = [psST, psST2]
            for s in range(TT + 1):
                ents = []
                if s <= TT - 1:
                    ents.append(("tensor", lambda e, tkb=s, tqn=tqn, pb=psS[s % 2]: e.matmul(
                        pb[:, 0:512], kh[:, tkb * 128:(tkb + 1) * 128],
                        qh[:, tqn * 512:(tqn + 1) * 512], start=True, stop=True)))
                if s >= 1:
                    tkb = s - 1
                    ents.append(("scalar", lambda e, tkb=tkb, pb=psS[tkb % 2]: e.activation(
                        est[:, tkb * 512:(tkb + 1) * 512], pb[:, 0:512], AF.Exp)))
                em.stage(ents)
            fns = []
            for tkb in range(TT):
                fns.append(lambda e, tkb=tkb: e.matmul(
                    psYT[0:64, 0:512], vh[:, tkb * 64:(tkb + 1) * 64],
                    est[:, tkb * 512:(tkb + 1) * 512],
                    start=(tkb == 0), stop=(tkb == TT - 1)))
                fns.append(lambda e, tkb=tkb: e.matmul(
                    psR1[0:1, 0:512], ones_b[:, 0:1],
                    est[:, tkb * 512:(tkb + 1) * 512],
                    start=(tkb == 0), stop=(tkb == TT - 1), skip_group_check=True))
            em.group("tensor", fns)
            em.op("vector", lambda e: e.reciprocal(rr[0:1, :], psR1[0:1, 0:512]))
            em.op("vector", lambda e: e.tensor_copy(rrb[0:1, :], rr[0:1, :]))
            em.op("tensor", lambda e: e.matmul(psBC[0:64, 0:512], ones_r[0:1, 0:64], rrb[0:1, :], start=True, stop=True))
            em.op("vector", lambda e: e.tensor_copy(bcs[:, :], psBC[0:64, 0:512]))
            em.op("vector", lambda e: e.tensor_tensor(ybf[:, :], psYT[0:64, 0:512], bcs[:, :], op=ALU.mult))
            em.dma(at[h * 64:(h + 1) * 64, tqn * 512:(tqn + 1) * 512], ybf[:, :])
    stB.close()

    stC = ExitStack()
    sbC = lambda nm, sh, dt: stC.enter_context(nc.sbuf_tensor(P + nm, sh, dt))
    aT = sbC("aT", [128, DG * N], BF16)
    wpst = sbC("wpst", [128, D], F32)
    wpbf = sbC("wpbf", [128, DG * D], BF16)
    xres = sbC("xres", [128, D], F32)
    s_ = sbC("s_", [128, D], F32)
    sqt = sbC("sqt", [128, D], F32)
    nsum = sbC("nsum", [128, 1], F32)
    ceps = sbC("ceps", [128, 1], F32)
    _ = em.op("vector", lambda e: e.memset(ceps[:, 0:1], EPS))
    wk1 = sbC("wk1", [128, 1], F32)
    o_ = sbC("o_", [128, D], F32)

    for g in range(DG):
        em.dma(aT[:, g * N:(g + 1) * N], at[g * 128:(g + 1) * 128, :])
    for kg in range(DG):
        em.dma(wpst[:], wproj[kg * 128:(kg + 1) * 128, :])
        em.op("vector", lambda e, kg=kg: e.tensor_copy(wpbf[:, kg * D:(kg + 1) * D], wpst[:]))
    for tt in range(TT):
        for nn in range(2):
            pt = ps0 if nn == 0 else ps1
            em.group("tensor", [
                (lambda e, kg=kg, tt=tt, nn=nn, pt=pt: e.matmul(
                    pt[:, 0:512], aT[:, kg * N + tt * 128: kg * N + (tt + 1) * 128],
                    wpbf[:, kg * D + nn * 512: kg * D + (nn + 1) * 512],
                    start=(kg == 0), stop=(kg == DG - 1)))
                for kg in range(DG)])
        em.dma(xres[:], x[tt * 128:(tt + 1) * 128, :])
        em.op("vector", lambda e: e.tensor_tensor(s_[:, 0:512], ps0[:, 0:512], xres[:, 0:512], op=ALU.add))
        em.op("vector", lambda e: e.tensor_tensor(s_[:, 512:1024], ps1[:, 0:512], xres[:, 512:1024], op=ALU.add))
        emit_ln(em, s_, o_, wk1, nsum, sqt, ceps)
        em.dma(x1[tt * 128:(tt + 1) * 128, :], o_[:])
    stC.close()
    st.close()


def emit_l2(nc, em, x1, rw, rb, lw, lb, w1, b1, w2, b2, idn, x2):
    P = "f_"
    st = ExitStack()
    sb = lambda nm, sh, dt: st.enter_context(nc.sbuf_tensor(P + nm, sh, dt))
    ps_ = lambda nm, sh, dt: st.enter_context(nc.psum_tensor(P + nm, sh, dt))

    idn_f = sb("idn_f", [128, 128], F32)
    ones_r = sb("ones_r", [1, 128], BF16)
    x1T = sb("x1T", [128, DG * N], BF16)
    xs = sb("xs", [128, D], F32)
    Wr = sb("Wr", [128, TT * 8], F32)
    lwT = sb("lwT", [128, DG * 4], BF16)
    rwT = sb("rwT", [128, DG * 8], BF16)
    blang = sb("blang", [128, 4], F32)
    brout = sb("brout", [128, 8], F32)
    rowf = sb("rowf", [1, 1024], F32)
    rowb = sb("rowb", [1, 1024], BF16)
    ll = sb("ll", [128, 8], F32)
    lp = sb("lp", [128, 4], F32)
    ew = sb("ew", [128, 4], F32)
    pp = sb("pp", [128, 8], F32)
    ge1 = sb("ge1", [128, 8], F32)
    m1 = sb("m1", [128, 1], F32)
    m2 = sb("m2", [128, 1], F32)
    wk1 = sb("wk1", [128, 1], F32)
    h_bf = sb("h_bf", [128, 32 * 512], BF16)
    b1sb = sb("b1sb", [128, 8 * 32], F32)
    b2bc = sb("b2bc", [128, D], F32)
    w1st = sb("w1st", [128, 2 * DG * 128], F32)
    w1bf = sb("w1bf", [128, 2 * DG * 128], BF16)
    w2st = sb("w2st", [128, 2 * D], F32)
    w2bf = sb("w2bf", [128, 2 * D], BF16)
    tz = sb("tz", [128, 512], F32)
    acc = sb("acc", [128, 4 * D], F32)
    xres = sb("xres", [128, D], F32)
    sqt = sb("sqt", [128, D], F32)
    nsum = sb("nsum", [128, 1], F32)
    ceps = sb("ceps", [128, 1], F32)
    _ = em.op("vector", lambda e: e.memset(ceps[:, 0:1], EPS))
    o_ = sb("o_", [128, D], F32)
    pz = [ps_("pz%d" % i, [128, 512], F32) for i in range(8)]

    em.dma(idn_f[:], idn[:, :])
    em.op("gpsimd", lambda e: e.memset(ones_r[0:1, :], 1.0))
    em.dma(xs[0:4, :], lw[:, :])
    for g in range(DG):
        em.op("tensor", lambda e, g=g: e.transpose(pz[0][:, 0:4], xs[0:4, g * 128:(g + 1) * 128], idn_f[0:4, 0:4]))
        em.op("vector", lambda e, g=g: e.tensor_copy(lwT[:, g * 4:(g + 1) * 4], pz[0][:, 0:4]))
    em.dma(xs[0:8, :], rw[:, :])
    for g in range(DG):
        em.op("tensor", lambda e, g=g: e.transpose(pz[0][:, 0:8], xs[0:8, g * 128:(g + 1) * 128], idn_f[0:8, 0:8]))
        em.op("vector", lambda e, g=g: e.tensor_copy(rwT[:, g * 8:(g + 1) * 8], pz[0][:, 0:8]))
    em.dma(rowf[0:1, 0:4], lb[:].rearrange("(o c) -> o c", o=1))
    em.op("vector", lambda e: e.tensor_copy(rowb[0:1, 0:4], rowf[0:1, 0:4]))
    em.op("tensor", lambda e: e.matmul(pz[0][:, 0:4], ones_r[0:1, 0:128], rowb[0:1, 0:4], start=True, stop=True))
    em.op("vector", lambda e: e.tensor_copy(blang[:], pz[0][:, 0:4]))
    em.dma(rowf[0:1, 0:8], rb[:].rearrange("(o c) -> o c", o=1))
    em.op("vector", lambda e: e.tensor_copy(rowb[0:1, 0:8], rowf[0:1, 0:8]))
    em.op("tensor", lambda e: e.matmul(pz[0][:, 0:8], ones_r[0:1, 0:128], rowb[0:1, 0:8], start=True, stop=True))
    em.op("vector", lambda e: e.tensor_copy(brout[:], pz[0][:, 0:8]))
    for e_ in range(8):
        em.dma_group([
            (b1sb[:, e_ * 32 + m: e_ * 32 + m + 1], b1[e_, m * 128:(m + 1) * 128].rearrange("(p o) -> p o", o=1))
            for m in range(32)])

    for tt in range(TT):
        em.dma(xs[:], x1[tt * 128:(tt + 1) * 128, :])
        for g in range(DG):
            em.op("tensor", lambda e, g=g: e.transpose(pz[1][:, 0:128], xs[:, g * 128:(g + 1) * 128], idn_f[:]))
            em.op("vector", lambda e, g=g, tt=tt: e.tensor_copy(x1T[:, g * N + tt * 128: g * N + tt * 128 + 128], pz[1][:, 0:128]))
        em.group("tensor", [
            (lambda e, g=g, tt=tt: e.matmul(pz[0][:, 0:4], x1T[:, g * N + tt * 128: g * N + (tt) * 128 + 128],
                                            lwT[:, g * 4:(g + 1) * 4], start=(g == 0), stop=(g == DG - 1)))
            for g in range(DG)])
        em.op("vector", lambda e: e.tensor_tensor(lp[:], pz[0][:, 0:4], blang[:], op=ALU.add))
        em.op("vector", lambda e: e.reduce_max(m1[:, 0:1], lp[:], axis=AX.X, negate=True))
        em.op("scalar", lambda e: e.activation(lp[:], lp[:], AF.Exp, bias=m1[:, 0:1], accum_out=nsum[:, 0:1]))
        em.op("vector", lambda e: e.reciprocal(wk1[:, 0:1], nsum[:, 0:1]))
        em.op("vector", lambda e: e.tensor_scalar(lp[:], lp[:], wk1[:, 0:1], None, op0=ALU.mult))
        em.op("vector", lambda e: e.tensor_tensor(ew[:, 0:3], lp[:, 0:3], lp[:, 1:4], op=ALU.add))
        em.op("vector", lambda e: e.tensor_copy(ew[:, 3:4], lp[:, 3:4]))
        em.group("tensor", [
            (lambda e, g=g, tt=tt: e.matmul(pz[0][:, 0:8], x1T[:, g * N + tt * 128: g * N + tt * 128 + 128],
                                            rwT[:, g * 8:(g + 1) * 8], start=(g == 0), stop=(g == DG - 1)))
            for g in range(DG)])
        em.op("vector", lambda e: e.tensor_tensor(ll[:], pz[0][:, 0:8], brout[:], op=ALU.add))
        em.op("vector", lambda e: e.scalar_tensor_tensor(ll[:, 0:4], ew[:], 0.1, ll[:, 0:4], op0=ALU.mult, op1=ALU.add))
        em.op("vector", lambda e: e.scalar_tensor_tensor(ll[:, 4:8], ew[:], 0.1, ll[:, 4:8], op0=ALU.mult, op1=ALU.add))
        em.op("vector", lambda e: e.reduce_max(m1[:, 0:1], ll[:], axis=AX.X, negate=True))
        em.op("scalar", lambda e: e.activation(pp[:], ll[:], AF.Exp, bias=m1[:, 0:1], accum_out=nsum[:, 0:1]))
        em.op("vector", lambda e: e.reciprocal(wk1[:, 0:1], nsum[:, 0:1]))
        em.op("vector", lambda e: e.tensor_scalar(pp[:], pp[:], wk1[:, 0:1], None, op0=ALU.mult))
        em.op("vector", lambda e: e.reduce_max(m1[:, 0:1], pp[:], axis=AX.X))
        em.op("vector", lambda e: e.tensor_scalar(ge1[:], pp[:], m1[:, 0:1], None, op0=ALU.is_ge))
        em.op("vector", lambda e: e.scalar_tensor_tensor(ge1[:], ge1[:], -2.0, pp[:], op0=ALU.mult, op1=ALU.add))
        em.op("vector", lambda e: e.reduce_max(m2[:, 0:1], ge1[:], axis=AX.X))
        em.op("vector", lambda e: e.tensor_tensor(wk1[:, 0:1], m1[:, 0:1], m2[:, 0:1], op=ALU.add))
        em.op("vector", lambda e: e.reciprocal(wk1[:, 0:1], wk1[:, 0:1]))
        em.op("vector", lambda e: e.tensor_scalar(ge1[:], pp[:], m2[:, 0:1], None, op0=ALU.is_ge))
        em.op("vector", lambda e: e.tensor_tensor(ge1[:], pp[:], ge1[:], op=ALU.mult))
        em.op("vector", lambda e, tt=tt: e.tensor_scalar(Wr[:, tt * 8:(tt + 1) * 8], ge1[:], wk1[:, 0:1], None, op0=ALU.mult))

    w1r = [w1[e_].rearrange("(g p) m -> p g m", p=128) for e_ in range(8)]
    for c in range(8):
        for e_ in range(8):
            em.dma(rowf[0:1, :], b2[e_:e_ + 1, :])
            em.op("vector", lambda e: e.tensor_copy(rowb[0:1, :], rowf[0:1, :]))
            em.op("tensor", lambda e: e.matmul(pz[0][:, 0:512], ones_r[0:1, 0:128], rowb[0:1, 0:512], start=True, stop=True))
            em.op("tensor", lambda e: e.matmul(pz[1][:, 0:512], ones_r[0:1, 0:128], rowb[0:1, 512:1024], start=True, stop=True))
            em.op("vector", lambda e: e.tensor_copy(b2bc[:, 0:512], pz[0][:, 0:512]))
            em.op("vector", lambda e: e.tensor_copy(b2bc[:, 512:1024], pz[1][:, 0:512]))
            for s in range(35):
                ents = []
                if s <= 31:
                    b = s % 2
                    ents.append(("sync", lambda e, m=s, b=b, e_=e_: e.dma_start(
                        w1st.rearrange("p (b g m) -> p b g m", b=2, g=DG)[:, b, :, :],
                        w1r[e_][:, :, m * 128:(m + 1) * 128])))
                if 1 <= s <= 32:
                    b = (s - 1) % 2
                    ents.append(("vector", lambda e, b=b: e.tensor_copy(
                        w1bf[:, b * 1024:(b + 1) * 1024], w1st[:, b * 1024:(b + 1) * 1024])))
                if 2 <= s <= 33:
                    m = s - 2
                    b = m % 2
                    pb = 6 + b
                    for kg in range(DG):
                        ents.append(("tensor", lambda e, kg=kg, c=c, b=b, pb=pb: e.matmul(
                            pz[pb][:, 0:512],
                            w1bf[:, b * 1024 + kg * 128: b * 1024 + (kg + 1) * 128],
                            x1T[:, kg * N + c * 512: kg * N + (c + 1) * 512],
                            start=(kg == 0), stop=(kg == DG - 1))))
                if 3 <= s <= 34:
                    m = s - 3
                    pb = 6 + (m % 2)
                    ents.append(("scalar", lambda e, m=m, e_=e_, pb=pb: e.activation(
                        h_bf[:, m * 512:(m + 1) * 512], pz[pb][:, 0:512], AF.Gelu,
                        bias=b1sb[:, e_ * 32 + m: e_ * 32 + m + 1])))
                em.stage(ents)
            for s in range(34):
                ents = []
                if s <= 31:
                    b = s % 2
                    ents.append(("sync", lambda e, kg=s, b=b, e_=e_: e.dma_start(
                        w2st[:, b * D:(b + 1) * D], w2[e_, kg * 128:(kg + 1) * 128, :])))
                if 1 <= s <= 32:
                    b = (s - 1) % 2
                    ents.append(("vector", lambda e, b=b: e.tensor_copy(
                        w2bf[:, b * D:(b + 1) * D], w2st[:, b * D:(b + 1) * D])))
                if 2 <= s <= 33:
                    kg = s - 2
                    b = kg % 2
                    for ts in range(4):
                        for nn in range(2):
                            ents.append(("tensor", lambda e, kg=kg, ts=ts, nn=nn, b=b: e.matmul(
                                pz[ts * 2 + nn][:, 0:512],
                                h_bf[:, kg * 512 + ts * 128: kg * 512 + (ts + 1) * 128],
                                w2bf[:, b * D + nn * 512: b * D + (nn + 1) * 512],
                                start=(kg == 0), stop=(kg == 31))))
                em.stage(ents)
            for ts in range(4):
                for nn in range(2):
                    em.op("vector", lambda e, ts=ts, nn=nn: e.tensor_tensor(
                        tz[:], pz[ts * 2 + nn][:, 0:512], b2bc[:, nn * 512:(nn + 1) * 512], op=ALU.add))
                    if e_ == 0:
                        em.op("vector", lambda e, ts=ts, nn=nn, c=c: e.tensor_scalar(
                            acc[:, ts * D + nn * 512: ts * D + (nn + 1) * 512], tz[:],
                            Wr[:, (c * 4 + ts) * 8 + 0: (c * 4 + ts) * 8 + 1], None, op0=ALU.mult))
                    else:
                        em.op("vector", lambda e, ts=ts, nn=nn, c=c, e_=e_: e.scalar_tensor_tensor(
                            acc[:, ts * D + nn * 512: ts * D + (nn + 1) * 512], tz[:],
                            Wr[:, (c * 4 + ts) * 8 + e_: (c * 4 + ts) * 8 + e_ + 1],
                            acc[:, ts * D + nn * 512: ts * D + (nn + 1) * 512], op0=ALU.mult, op1=ALU.add))
        for ts in range(4):
            tt = c * 4 + ts
            em.dma(xres[:], x1[tt * 128:(tt + 1) * 128, :])
            em.op("vector", lambda e, ts=ts: e.tensor_tensor(xres[:], xres[:], acc[:, ts * D:(ts + 1) * D], op=ALU.add))
            emit_ln(em, xres, o_, wk1, nsum, sqt, ceps)
            em.dma(x2[tt * 128:(tt + 1) * 128, :], o_[:])
    st.close()


def emit_l3(nc, em, x2, nw, nb, ipw, ipb, cw, cb, xpw, xpb, dpw, dpb, alog, dpar, opw, opb, idn, x3):
    P = "m_"
    ipT = nc.dram_tensor(P + "ipT", [D, 2 * DI], BF16, kind="Internal").ap()
    dpT = nc.dram_tensor(P + "dpT", [DI, DI], BF16, kind="Internal").ap()
    opT = nc.dram_tensor(P + "opT", [DI, D], BF16, kind="Internal").ap()

    st = ExitStack()
    sb = lambda nm, sh, dt: st.enter_context(nc.sbuf_tensor(P + nm, sh, dt))
    ps_ = lambda nm, sh, dt: st.enter_context(nc.psum_tensor(P + nm, sh, dt))

    CH = 512
    NCH = N // CH

    idn_f = sb("idn_f", [128, 128], F32)
    ones_r = sb("ones_r", [1, 128], BF16)
    stg = sb("stg", [128, 1024], F32)
    tbf = sb("tbf", [128, 128], BF16)
    nw_bc = sb("nw_bc", [128, D], F32)
    nb_bc = sb("nb_bc", [128, D], F32)
    ob_bc = sb("ob_bc", [128, D], F32)
    rowf = sb("rowf", [1, 1024], F32)
    rowb = sb("rowb", [1, 1024], BF16)
    ipb_sb = sb("ipb_sb", [128, 32], F32)
    cw_sb = sb("cw_sb", [128, DIG * 4], F32)
    cb_sb = sb("cb_sb", [128, DIG], F32)
    xpb_sb = sb("xpb_sb", [32, 1], F32)
    dpb_sb = sb("dpb_sb", [128, DIG], F32)
    an_sb = sb("an_sb", [128, DIG * DS], F32)
    dp_sb = sb("dp_sb", [128, DIG], F32)
    xps_bf = sb("xps_bf", [128, DIG * 32], BF16)
    hstate = sb("hstate", [128, DIG * DS], F32)
    xnT = sb("xnT", [128, DG * CH], BF16)
    xc_bf = sb("xc_bf", [128, DIG * (CH + 3)], BF16)
    z_bf = sb("z_bf", [128, DIG * CH], BF16)
    u_bf = sb("u_bf", [128, DIG * CH], BF16)
    delta = sb("delta", [128, DIG * CH], BF16)
    du_bf = sb("du_bf", [128, DIG * CH], BF16)
    bcB = sb("bcB", [128, DS * CH], BF16)
    bcC = sb("bcC", [128, DS * CH], BF16)
    wstr = sb("wstr", [128, 16 * 128], BF16)
    sp0 = sb("sp0", [128, CH], F32)
    sp1 = sb("sp1", [128, CH], F32)
    sp2 = sb("sp2", [128, CH], F32)
    xdb = sb("xdb", [32, CH], F32)
    rws = sb("rws", [1, CH], F32)
    rwb = sb("rwb", [1, CH], BF16)
    dab = sb("dab", [128, CH], BF16)
    dbu = sb("dbu", [128, CH], BF16)
    hsc = sb("hsc", [128, CH], BF16)
    yg = sb("yg", [128, CH], F32)
    ybf = z_bf
    otst = delta
    ytmp = sb("ytmp", [128, CH], F32)
    zsil = sb("zsil", [128, CH], F32)
    s_tm = sb("s_tm", [128, D], F32)
    xres = sb("xres", [128, D], F32)
    sqt = sb("sqt", [128, D], F32)
    nsum = sb("nsum", [128, 1], F32)
    ceps = sb("ceps", [128, 1], F32)
    _ = em.op("vector", lambda e: e.memset(ceps[:, 0:1], EPS))
    wk1 = sb("wk1", [128, 1], F32)
    o_ = sb("o_", [128, D], F32)
    xn_tm = sb("xn_tm", [128, D], F32)
    cone = sb("cone", [128, 1], F32)
    p0 = ps_("p0", [128, 512], F32)
    p1 = ps_("p1", [128, 512], F32)
    p2 = ps_("p2", [128, 512], F32)
    p3 = ps_("p3", [128, 512], F32)

    em.dma(idn_f[:], idn[:, :])
    em.op("gpsimd", lambda e: e.memset(ones_r[0:1, :], 1.0))
    em.op("vector", lambda e: e.memset(hstate[:], 0.0))
    em.op("vector", lambda e: e.memset(cone[:, 0:1], 1.0))
    em.op("vector", lambda e: e.memset(xc_bf[:], 0.0))

    def prep_transpose(src, rows, cols, dstT):
        for rtile in range(rows // 128):
            for ch in range(0, cols, 1024):
                cw_ = min(1024, cols - ch)
                em.dma(stg[:, 0:cw_], src[rtile * 128:(rtile + 1) * 128, ch:ch + cw_])
                for g in range(cw_ // 128):
                    em.op("tensor", lambda e, g=g: e.transpose(p0[:, 0:128], stg[:, g * 128:(g + 1) * 128], idn_f[:]))
                    em.op("vector", lambda e: e.tensor_copy(tbf[:], p0[:, 0:128]))
                    em.dma(dstT[ch + g * 128:ch + (g + 1) * 128, rtile * 128:(rtile + 1) * 128], tbf[:])

    prep_transpose(ipw, 2 * DI, D, ipT)
    prep_transpose(dpw, DI, DI, dpT)
    prep_transpose(opw, D, DI, opT)
    for ch in range(2):
        em.dma(stg[0:32, 0:1024], xpw[:, ch * 1024:(ch + 1) * 1024])
        for g in range(8):
            em.op("tensor", lambda e, g=g: e.transpose(p0[:, 0:32], stg[0:32, g * 128:(g + 1) * 128], idn_f[0:32, 0:32]))
            em.op("vector", lambda e, g=g, ch=ch: e.tensor_copy(xps_bf[:, (ch * 8 + g) * 32:(ch * 8 + g + 1) * 32], p0[:, 0:32]))

    def bcast_row(dst, src_row1xn, width):
        em.op("vector", lambda e: e.tensor_copy(rowb[0:1, 0:width], src_row1xn))
        nchunks = (width + 511) // 512
        for q in range(nchunks):
            w = min(512, width - q * 512)
            em.op("tensor", lambda e, q=q, w=w: e.matmul(p0[:, 0:w], ones_r[0:1, 0:128], rowb[0:1, q * 512:q * 512 + w], start=True, stop=True))
            em.op("vector", lambda e, q=q, w=w, dst=dst: e.tensor_copy(dst[:, q * 512:q * 512 + w], p0[:, 0:w]))

    em.dma(rowf[0:1, :], nw[:].rearrange("(o c) -> o c", o=1))
    bcast_row(nw_bc, rowf[0:1, 0:D], D)
    em.dma(rowf[0:1, :], nb[:].rearrange("(o c) -> o c", o=1))
    bcast_row(nb_bc, rowf[0:1, 0:D], D)
    em.dma(rowf[0:1, :], opb[:].rearrange("(o c) -> o c", o=1))
    bcast_row(ob_bc, rowf[0:1, 0:D], D)
    em.dma_group([(ipb_sb[:, m:m + 1], ipb[m * 128:(m + 1) * 128].rearrange("(p o) -> p o", o=1)) for m in range(32)])
    em.dma_group([(cw_sb.rearrange("p (g k) -> p g k", k=4)[:, g, :], cw[g * 128:(g + 1) * 128, 0, :]) for g in range(DIG)])
    em.dma_group([(cb_sb[:, g:g + 1], cb[g * 128:(g + 1) * 128].rearrange("(p o) -> p o", o=1)) for g in range(DIG)])
    em.dma(xpb_sb[0:32, 0:1], xpb[:].rearrange("(p o) -> p o", o=1))
    em.dma_group([(dpb_sb[:, g:g + 1], dpb[g * 128:(g + 1) * 128].rearrange("(p o) -> p o", o=1)) for g in range(DIG)])
    em.dma_group([(dp_sb[:, g:g + 1], dpar[g * 128:(g + 1) * 128].rearrange("(p o) -> p o", o=1)) for g in range(DIG)])
    em.dma_group([(an_sb[:, g * DS:(g + 1) * DS], alog[g * 128:(g + 1) * 128, :]) for g in range(DIG)])
    em.op("scalar", lambda e: e.activation(an_sb[:], an_sb[:], AF.Exp))
    em.op("vector", lambda e: e.tensor_scalar(an_sb[:], an_sb[:], -1.0, None, op0=ALU.mult))

    ipT_r = ipT.rearrange("(g p) m -> p g m", p=128)
    dpT_r = dpT.rearrange("(g p) m -> p g m", p=128)
    opT_r = opT.rearrange("(g p) m -> p g m", p=128)

    for c in range(NCH):
        for ts in range(CH // 128):
            tt = c * 4 + ts
            em.dma(xres[:], x2[tt * 128:(tt + 1) * 128, :])
            em.op("vector", lambda e: e.tensor_copy(s_tm[:], xres[:]))
            emit_ln(em, s_tm, xn_tm, wk1, nsum, sqt, ceps)
            em.op("vector", lambda e: e.tensor_tensor(xn_tm[:], xn_tm[:], nw_bc[:], op=ALU.mult))
            em.op("vector", lambda e: e.tensor_tensor(xn_tm[:], xn_tm[:], nb_bc[:], op=ALU.add))
            for g in range(DG):
                em.op("tensor", lambda e, g=g: e.transpose(p0[:, 0:128], xn_tm[:, g * 128:(g + 1) * 128], idn_f[:]))
                em.op("vector", lambda e, g=g, ts=ts: e.tensor_copy(xnT[:, g * CH + ts * 128: g * CH + (ts + 1) * 128], p0[:, 0:128]))
        for m in range(32):
            em.dma(wstr[:, 0:DG * 128].rearrange("p (g m) -> p g m", g=DG), ipT_r[:, :, m * 128:(m + 1) * 128])
            em.group("tensor", [
                (lambda e, kg=kg: e.matmul(p1[:, 0:CH], wstr[:, kg * 128:(kg + 1) * 128],
                                           xnT[:, kg * CH:(kg + 1) * CH], start=(kg == 0), stop=(kg == DG - 1)))
                for kg in range(DG)])
            if m < 16:
                em.op("vector", lambda e, m=m: e.tensor_scalar(
                    xc_bf.rearrange("p (g t) -> p g t", t=CH + 3)[:, m, 3:CH + 3],
                    p1[:, 0:CH], ipb_sb[:, m:m + 1], None, op0=ALU.add))
            else:
                em.op("vector", lambda e, m=m: e.tensor_scalar(
                    z_bf.rearrange("p (g t) -> p g t", t=CH)[:, m - 16, :],
                    p1[:, 0:CH], ipb_sb[:, m:m + 1], None, op0=ALU.add))
        xc_r = xc_bf.rearrange("p (g t) -> p g t", t=CH + 3)
        u_r = u_bf.rearrange("p (g t) -> p g t", t=CH)
        for g in range(DIG):
            cwr = cw_sb.rearrange("p (g k) -> p g k", k=4)
            em.op("vector", lambda e, g=g: e.tensor_scalar(sp1[:], xc_r[:, g, 0:CH], cwr[:, g, 0:1], None, op0=ALU.mult))
            for k in range(1, 4):
                em.op("vector", lambda e, g=g, k=k: e.scalar_tensor_tensor(
                    sp1[:], xc_r[:, g, k:k + CH], cwr[:, g, k:k + 1], sp1[:], op0=ALU.mult, op1=ALU.add))
            em.op("scalar", lambda e, g=g: e.activation(u_r[:, g, :], sp1[:], AF.Silu, bias=cb_sb[:, g:g + 1]))
            em.op("vector", lambda e, g=g: e.tensor_copy(xc_r[:, g, 0:3], xc_r[:, g, CH:CH + 3]))
        em.group("tensor", [
            (lambda e, kg=kg: e.matmul(p1[0:32, 0:CH], xps_bf[:, kg * 32:(kg + 1) * 32],
                                       u_r[:, kg, :], start=(kg == 0), stop=(kg == DIG - 1)))
            for kg in range(DIG)])
        em.op("vector", lambda e: e.tensor_scalar(xdb[0:32, :], p1[0:32, 0:CH], xpb_sb[0:32, 0:1], None, op0=ALU.add))
        for s in range(DS):
            for (src_row, dst) in ((s, bcB), (16 + s, bcC)):
                em.dma(rws[0:1, :], xdb[src_row:src_row + 1, :])
                em.op("vector", lambda e: e.tensor_copy(rwb[0:1, 0:CH], rws[0:1, :]))
                em.op("tensor", lambda e: e.matmul(p2[:, 0:CH], ones_r[0:1, 0:128], rwb[0:1, 0:CH], start=True, stop=True))
                em.op("vector", lambda e, s=s, dst=dst: e.tensor_copy(
                    dst.rearrange("p (s t) -> p s t", t=CH)[:, s, :], p2[:, 0:CH]))
        d_r = delta.rearrange("p (g t) -> p g t", t=CH)
        du_r = du_bf.rearrange("p (g t) -> p g t", t=CH)
        for g in range(DIG):
            em.dma(wstr[:].rearrange("p (g m) -> p g m", g=DIG)[:, 0:8, :], dpT_r[:, 0:8, g * 128:(g + 1) * 128])
            em.dma(wstr[:].rearrange("p (g m) -> p g m", g=DIG)[:, 8:16, :], dpT_r[:, 8:16, g * 128:(g + 1) * 128])
            em.group("tensor", [
                (lambda e, kg=kg: e.matmul(p1[:, 0:CH], wstr[:, kg * 128:(kg + 1) * 128],
                                           u_r[:, kg, :], start=(kg == 0), stop=(kg == DIG - 1)))
                for kg in range(DIG)])
            em.op("vector", lambda e, g=g: e.tensor_scalar(sp0[:], p1[:, 0:CH], dpb_sb[:, g:g + 1], None, op0=ALU.add))
            em.op("vector", lambda e: e.tensor_scalar(sp1[:], sp0[:], 0.0, None, op0=ALU.max))
            em.op("scalar", lambda e: e.activation(sp2[:], sp0[:], AF.Abs))
            em.op("scalar", lambda e: e.activation(sp2[:], sp2[:], AF.Exp, scale=-1.0))
            em.op("scalar", lambda e: e.activation(sp2[:], sp2[:], AF.Ln, bias=cone[:, 0:1]))
            em.op("vector", lambda e, g=g: e.tensor_tensor(d_r[:, g, :], sp1[:], sp2[:], op=ALU.add))
            em.op("vector", lambda e, g=g: e.tensor_tensor(du_r[:, g, :], d_r[:, g, :], u_r[:, g, :], op=ALU.mult))
        bcB_r = bcB.rearrange("p (s t) -> p s t", t=CH)
        bcC_r = bcC.rearrange("p (s t) -> p s t", t=CH)
        y_r = ybf.rearrange("p (g t) -> p g t", t=CH)
        for g in range(DIG):
            for s in range(DS):
                em.op("scalar", lambda e, g=g, s=s: e.activation(dab[:], d_r[:, g, :], AF.Exp, scale=an_sb[:, g * DS + s: g * DS + s + 1]))
                em.op("vector", lambda e, g=g, s=s: e.tensor_tensor(dbu[:], du_r[:, g, :], bcB_r[:, s, :], op=ALU.mult))
                em.op("vector", lambda e, g=g, s=s: e.tensor_tensor_scan(
                    hsc[:], dab[:], dbu[:], hstate[:, g * DS + s: g * DS + s + 1], op0=ALU.mult, op1=ALU.add))
                em.op("vector", lambda e, g=g, s=s: e.tensor_copy(hstate[:, g * DS + s: g * DS + s + 1], hsc[:, CH - 1:CH]))
                if s == 0:
                    em.op("vector", lambda e, g=g, s=s: e.tensor_tensor(yg[:], hsc[:], bcC_r[:, s, :], op=ALU.mult))
                else:
                    em.op("vector", lambda e, g=g, s=s: e.tensor_tensor(ytmp[:], hsc[:], bcC_r[:, s, :], op=ALU.mult))
                    em.op("vector", lambda e: e.tensor_tensor(yg[:], yg[:], ytmp[:], op=ALU.add))
            em.op("vector", lambda e, g=g: e.scalar_tensor_tensor(yg[:], u_r[:, g, :], dp_sb[:, g:g + 1], yg[:], op0=ALU.mult, op1=ALU.add))
            em.op("scalar", lambda e, g=g: e.activation(zsil[:], z_bf.rearrange("p (g t) -> p g t", t=CH)[:, g, :], AF.Silu))
            em.op("vector", lambda e, g=g: e.tensor_tensor(y_r[:, g, :], yg[:], zsil[:], op=ALU.mult))
        du_st = du_bf.rearrange("p (a b) -> p a b", a=16)
        for nn in range(2):
            for q in range(4):
                em.dma(otst[:].rearrange("p (g m) -> p g m", g=DIG)[:, q * 4:(q + 1) * 4, :],
                       opT_r[:, q * 4:(q + 1) * 4, nn * 512:(nn + 1) * 512])
            for ts in range(4):
                em.group("tensor", [
                    (lambda e, kg=kg, ts=ts: e.matmul(p3[:, 0:512],
                                                      y_r[:, kg, ts * 128:(ts + 1) * 128],
                                                      otst[:, kg * 512:(kg + 1) * 512],
                                                      start=(kg == 0), stop=(kg == DIG - 1)))
                    for kg in range(DIG)])
                em.op("vector", lambda e, ts=ts, nn=nn: e.tensor_tensor(
                    du_st[:, ts * 2 + nn, 0:512], p3[:, 0:512],
                    ob_bc[:, nn * 512:(nn + 1) * 512], op=ALU.add))
        for ts in range(4):
            tt = c * 4 + ts
            em.dma(xres[:], x2[tt * 128:(tt + 1) * 128, :])
            for nn in range(2):
                em.op("vector", lambda e, ts=ts, nn=nn: e.scalar_tensor_tensor(
                    s_tm[:, nn * 512:(nn + 1) * 512], xres[:, nn * 512:(nn + 1) * 512], 2.0,
                    du_st[:, ts * 2 + nn, 0:512], op0=ALU.mult, op1=ALU.add))
            emit_ln(em, s_tm, o_, wk1, nsum, sqt, ceps)
            em.dma(x3[tt * 128:(tt + 1) * 128, :], o_[:])
    st.close()


def emit_l4(nc, em, x3, rW, rb, sW1, sb1, sW2, sb2, eW1, eb1, eW2, eb2, idn, out):
    P = "g_"
    E = 15
    H = 1024
    st = ExitStack()
    sb_ = lambda nm, sh, dt: st.enter_context(nc.sbuf_tensor(P + nm, sh, dt))
    ps_ = lambda nm, sh, dt: st.enter_context(nc.psum_tensor(P + nm, sh, dt))

    idn_f = sb_("idn_f", [128, 128], F32)
    ones_r = sb_("ones_r", [1, 128], BF16)
    xs = sb_("xs", [128, D], F32)
    x3T = sb_("x3T", [128, DG * 512], BF16)
    rwbf = sb_("rwbf", [128, DG * E], BF16)
    brb = sb_("brb", [128, E], F32)
    sw1bf = sb_("sw1bf", [128, DG * H], BF16)
    sw2bf = sb_("sw2bf", [128, DG * D], BF16)
    sb1p = sb_("sb1p", [128, 8], F32)
    sb2bc = sb_("sb2bc", [128, D], F32)
    eb1p = sb_("eb1p", [128, E * 8], F32)
    eb2bc = sb_("eb2bc", [128, D], F32)
    rowf = sb_("rowf", [1, 1024], F32)
    rowb = sb_("rowb", [1, 1024], BF16)
    wst = sb_("wst", [128, 2048], F32)
    wbf = sb_("wbf", [128, 2048], BF16)
    hsh = sb_("hsh", [128, 8 * 512], BF16)
    he = sb_("he", [128, 8 * 512], BF16)
    Wr = sb_("Wr", [128, 4 * 16], F32)
    pp = sb_("pp", [128, E], F32)
    ge1 = sb_("ge1", [128, E], F32)
    m1 = sb_("m1", [128, 1], F32)
    m2 = sb_("m2", [128, 1], F32)
    acc = sb_("acc", [128, 4 * D], F32)
    tz = sb_("tz", [128, 512], F32)
    xres = sb_("xres", [128, D], F32)
    sqt = sb_("sqt", [128, D], F32)
    nsum = sb_("nsum", [128, 1], F32)
    ceps = sb_("ceps", [128, 1], F32)
    _ = em.op("vector", lambda e: e.memset(ceps[:, 0:1], EPS))
    wk1 = sb_("wk1", [128, 1], F32)
    o_ = sb_("o_", [128, D], F32)
    pz = [ps_("pz%d" % i, [128, 512], F32) for i in range(8)]

    em.dma(idn_f[:], idn[:, :])
    em.op("gpsimd", lambda e: e.memset(ones_r[0:1, :], 1.0))

    def bcast_row(dst, width):
        em.op("vector", lambda e: e.tensor_copy(rowb[0:1, 0:width], rowf[0:1, 0:width]))
        for q in range((width + 511) // 512):
            w = min(512, width - q * 512)
            em.op("tensor", lambda e, q=q, w=w: e.matmul(pz[6][:, 0:w], ones_r[0:1, 0:128], rowb[0:1, q * 512:q * 512 + w], start=True, stop=True))
            em.op("vector", lambda e, q=q, w=w, dst=dst: e.tensor_copy(dst[:, q * 512:q * 512 + w], pz[6][:, 0:w]))

    for g in range(DG):
        em.dma(wst[:, 0:E], rW[g * 128:(g + 1) * 128, :])
        em.op("vector", lambda e, g=g: e.tensor_copy(rwbf[:, g * E:(g + 1) * E], wst[:, 0:E]))
    em.dma(rowf[0:1, 0:E], rb[:].rearrange("(o c) -> o c", o=1))
    bcast_row(brb, E)
    em.dma(rowf[0:1, :], sb2[:].rearrange("(o c) -> o c", o=1))
    bcast_row(sb2bc, D)
    em.dma_group([(sb1p[:, m:m + 1], sb1[m * 128:(m + 1) * 128].rearrange("(p o) -> p o", o=1)) for m in range(8)])
    for e_ in range(E):
        em.dma_group([(eb1p[:, e_ * 8 + m: e_ * 8 + m + 1], eb1[e_, m * 128:(m + 1) * 128].rearrange("(p o) -> p o", o=1)) for m in range(8)])
    for kg in range(DG):
        em.dma(wst[:, 0:H], sW1[kg * 128:(kg + 1) * 128, :])
        em.op("vector", lambda e, kg=kg: e.tensor_copy(sw1bf[:, kg * H:(kg + 1) * H], wst[:, 0:H]))
        em.dma(wst[:, 0:D], sW2[kg * 128:(kg + 1) * 128, :])
        em.op("vector", lambda e, kg=kg: e.tensor_copy(sw2bf[:, kg * D:(kg + 1) * D], wst[:, 0:D]))

    for c in range(8):
        for ts in range(4):
            tt = c * 4 + ts
            em.dma(xs[:], x3[tt * 128:(tt + 1) * 128, :])
            for g in range(DG):
                em.op("tensor", lambda e, g=g: e.transpose(pz[7][:, 0:128], xs[:, g * 128:(g + 1) * 128], idn_f[:]))
                em.op("vector", lambda e, g=g, ts=ts: e.tensor_copy(x3T[:, g * 512 + ts * 128: g * 512 + (ts + 1) * 128], pz[7][:, 0:128]))
            em.group("tensor", [
                (lambda e, g=g, ts=ts: e.matmul(pz[0][:, 0:E], x3T[:, g * 512 + ts * 128: g * 512 + ts * 128 + 128],
                                                rwbf[:, g * E:(g + 1) * E], start=(g == 0), stop=(g == DG - 1)))
                for g in range(DG)])
            em.op("vector", lambda e: e.tensor_tensor(pp[:], pz[0][:, 0:E], brb[:], op=ALU.add))
            em.op("vector", lambda e: e.reduce_max(m1[:, 0:1], pp[:], axis=AX.X, negate=True))
            em.op("scalar", lambda e: e.activation(pp[:], pp[:], AF.Exp, bias=m1[:, 0:1], accum_out=nsum[:, 0:1]))
            em.op("vector", lambda e: e.reciprocal(wk1[:, 0:1], nsum[:, 0:1]))
            em.op("vector", lambda e: e.tensor_scalar(pp[:], pp[:], wk1[:, 0:1], None, op0=ALU.mult))
            em.op("vector", lambda e: e.reduce_max(m1[:, 0:1], pp[:], axis=AX.X))
            em.op("vector", lambda e: e.tensor_scalar(ge1[:], pp[:], m1[:, 0:1], None, op0=ALU.is_ge))
            em.op("vector", lambda e: e.scalar_tensor_tensor(ge1[:], ge1[:], -2.0, pp[:], op0=ALU.mult, op1=ALU.add))
            em.op("vector", lambda e: e.reduce_max(m2[:, 0:1], ge1[:], axis=AX.X))
            em.op("vector", lambda e: e.tensor_scalar(sqt[:, 0:E], ge1[:], m2[:, 0:1], None, op0=ALU.is_ge))
            em.op("vector", lambda e: e.scalar_tensor_tensor(ge1[:], sqt[:, 0:E], -2.0, ge1[:], op0=ALU.mult, op1=ALU.add))
            em.op("vector", lambda e: e.reduce_max(m2[:, 0:1], ge1[:], axis=AX.X))
            em.op("vector", lambda e: e.tensor_scalar(ge1[:], pp[:], m2[:, 0:1], None, op0=ALU.is_ge))
            em.op("vector", lambda e, ts=ts: e.tensor_tensor(Wr[:, ts * 16: ts * 16 + E], pp[:], ge1[:], op=ALU.mult))
        for m in range(8):
            em.group("tensor", [
                (lambda e, kg=kg, m=m: e.matmul(pz[0][:, 0:512], sw1bf[:, kg * H + m * 128: kg * H + (m + 1) * 128],
                                                x3T[:, kg * 512:(kg + 1) * 512], start=(kg == 0), stop=(kg == DG - 1)))
                for kg in range(DG)])
            em.op("scalar", lambda e, m=m: e.activation(hsh[:, m * 512:(m + 1) * 512], pz[0][:, 0:512], AF.Relu, bias=sb1p[:, m:m + 1]))
        fns = []
        for kg in range(8):
            for ts in range(4):
                for nn in range(2):
                    fns.append(lambda e, kg=kg, ts=ts, nn=nn: e.matmul(
                        pz[ts * 2 + nn][:, 0:512],
                        hsh[:, kg * 512 + ts * 128: kg * 512 + (ts + 1) * 128],
                        sw2bf[:, kg * D + nn * 512: kg * D + (nn + 1) * 512],
                        start=(kg == 0), stop=(kg == 7)))
        em.group("tensor", fns)
        for ts in range(4):
            for nn in range(2):
                em.op("vector", lambda e, ts=ts, nn=nn: e.tensor_tensor(
                    acc[:, ts * D + nn * 512: ts * D + (nn + 1) * 512],
                    pz[ts * 2 + nn][:, 0:512], sb2bc[:, nn * 512:(nn + 1) * 512], op=ALU.add))
        for e_ in range(E):
            em.dma(rowf[0:1, :], eb2[e_:e_ + 1, :])
            bcast_row(eb2bc, D)
            for s in range(11):
                ents = []
                if s <= 7:
                    b = s % 2
                    ents.append(("sync", lambda e, m=s, b=b, e_=e_: e.dma_start(
                        wst.rearrange("p (b g m) -> p b g m", b=2, g=DG)[:, b, :, :],
                        eW1[e_].rearrange("(g p) m -> p g m", p=128)[:, :, m * 128:(m + 1) * 128])))
                if 1 <= s <= 8:
                    b = (s - 1) % 2
                    ents.append(("vector", lambda e, b=b: e.tensor_copy(
                        wbf[:, b * 1024:(b + 1) * 1024], wst[:, b * 1024:(b + 1) * 1024])))
                if 2 <= s <= 9:
                    m = s - 2
                    b = m % 2
                    pb = 6 + b
                    for kg in range(DG):
                        ents.append(("tensor", lambda e, kg=kg, b=b, pb=pb: e.matmul(
                            pz[pb][:, 0:512],
                            wbf[:, b * 1024 + kg * 128: b * 1024 + (kg + 1) * 128],
                            x3T[:, kg * 512:(kg + 1) * 512],
                            start=(kg == 0), stop=(kg == DG - 1))))
                if 3 <= s <= 10:
                    m = s - 3
                    pb = 6 + (m % 2)
                    ents.append(("scalar", lambda e, m=m, e_=e_, pb=pb: e.activation(
                        he[:, m * 512:(m + 1) * 512], pz[pb][:, 0:512], AF.Relu,
                        bias=eb1p[:, e_ * 8 + m: e_ * 8 + m + 1])))
                em.stage(ents)
            for s in range(10):
                ents = []
                if s <= 7:
                    b = s % 2
                    ents.append(("sync", lambda e, kg=s, b=b, e_=e_: e.dma_start(
                        wst[:, b * 1024:(b + 1) * 1024], eW2[e_, kg * 128:(kg + 1) * 128, :])))
                if 1 <= s <= 8:
                    b = (s - 1) % 2
                    ents.append(("vector", lambda e, b=b: e.tensor_copy(
                        wbf[:, b * 1024:(b + 1) * 1024], wst[:, b * 1024:(b + 1) * 1024])))
                if 2 <= s <= 9:
                    kg = s - 2
                    b = kg % 2
                    for ts in range(4):
                        for nn in range(2):
                            ents.append(("tensor", lambda e, kg=kg, ts=ts, nn=nn, b=b: e.matmul(
                                pz[ts * 2 + nn][:, 0:512],
                                he[:, kg * 512 + ts * 128: kg * 512 + (ts + 1) * 128],
                                wbf[:, b * 1024 + nn * 512: b * 1024 + (nn + 1) * 512],
                                start=(kg == 0), stop=(kg == 7))))
                em.stage(ents)
            for ts in range(4):
                for nn in range(2):
                    em.op("vector", lambda e, ts=ts, nn=nn: e.tensor_tensor(
                        tz[:], pz[ts * 2 + nn][:, 0:512], eb2bc[:, nn * 512:(nn + 1) * 512], op=ALU.add))
                    em.op("vector", lambda e, ts=ts, nn=nn, e_=e_: e.scalar_tensor_tensor(
                        acc[:, ts * D + nn * 512: ts * D + (nn + 1) * 512], tz[:],
                        Wr[:, ts * 16 + e_: ts * 16 + e_ + 1],
                        acc[:, ts * D + nn * 512: ts * D + (nn + 1) * 512], op0=ALU.mult, op1=ALU.add))
        for ts in range(4):
            tt = c * 4 + ts
            em.dma(xres[:], x3[tt * 128:(tt + 1) * 128, :])
            em.op("vector", lambda e, ts=ts: e.tensor_tensor(xres[:], xres[:], acc[:, ts * D:(ts + 1) * D], op=ALU.add))
            emit_ln(em, xres, o_, wk1, nsum, sqt, ceps)
            em.dma(out[tt * 128:(tt + 1) * 128, :], o_[:])
    st.close()


def build_fused():
    nc = bass.Bass("TRN2", target_bir_lowering=False)
    d = lambda nm, sh: nc.dram_tensor(nm, sh, F32, kind="ExternalInput").ap()
    x = d("x", [N, D])
    wqkv = d("wqkv", [D, 3 * D])
    wproj = d("wproj", [D, D])
    cosq = d("cosq", [128, N])
    sinq = d("sinq", [128, N])
    cosk = d("cosk", [128, N])
    sink = d("sink", [128, N])
    rt = d("rt", [128, 128])
    idn = d("idn", [128, 128])
    f_rw = d("f_rw", [8, D])
    f_rb = d("f_rb", [8])
    f_lw = d("f_lw", [4, D])
    f_lb = d("f_lb", [4])
    f_w1 = d("f_w1", [8, D, 4096])
    f_b1 = d("f_b1", [8, 4096])
    f_w2 = d("f_w2", [8, 4096, D])
    f_b2 = d("f_b2", [8, D])
    m_nw = d("m_nw", [D])
    m_nb = d("m_nb", [D])
    m_ipw = d("m_ipw", [2 * DI, D])
    m_ipb = d("m_ipb", [2 * DI])
    m_cw = d("m_cw", [DI, 1, 4])
    m_cb = d("m_cb", [DI])
    m_xpw = d("m_xpw", [2 * DS, DI])
    m_xpb = d("m_xpb", [2 * DS])
    m_dpw = d("m_dpw", [DI, DI])
    m_dpb = d("m_dpb", [DI])
    m_alog = d("m_alog", [DI, DS])
    m_dpar = d("m_dpar", [DI])
    m_opw = d("m_opw", [D, DI])
    m_opb = d("m_opb", [D])
    g_rW = d("g_rW", [D, 15])
    g_rb = d("g_rb", [15])
    g_sW1 = d("g_sW1", [D, 1024])
    g_sb1 = d("g_sb1", [1024])
    g_sW2 = d("g_sW2", [1024, D])
    g_sb2 = d("g_sb2", [D])
    g_eW1 = d("g_eW1", [15, D, 1024])
    g_eb1 = d("g_eb1", [15, 1024])
    g_eW2 = d("g_eW2", [15, 1024, D])
    g_eb2 = d("g_eb2", [15, D])
    out = nc.dram_tensor("out", [N, D], F32, kind="ExternalOutput").ap()
    x1 = nc.dram_tensor("x1", [N, D], F32, kind="Internal").ap()
    x2 = nc.dram_tensor("x2", [N, D], F32, kind="Internal").ap()
    x3 = nc.dram_tensor("x3", [N, D], F32, kind="Internal").ap()

    em = Em()
    emit_l1(nc, em, x, wqkv, wproj, cosq, sinq, cosk, sink, rt, idn, x1)
    emit_l2(nc, em, x1, f_rw, f_rb, f_lw, f_lb, f_w1, f_b1, f_w2, f_b2, idn, x2)
    emit_l3(nc, em, x2, m_nw, m_nb, m_ipw, m_ipb, m_cw, m_cb, m_xpw, m_xpb, m_dpw, m_dpb,
            m_alog, m_dpar, m_opw, m_opb, idn, x3)
    emit_l4(nc, em, x3, g_rW, g_rb, g_sW1, g_sb1, g_sW2, g_sb2, g_eW1, g_eb1, g_eW2, g_eb2, idn, out)
    with nc.semaphore("tok") as tok:
        em.build(nc, tok)
    return nc


_CACHE = {}


def _get(name, builder):
    if name not in _CACHE:
        _CACHE[name] = builder()
    return _CACHE[name]


def kernel(**inp):
    cosq, sinq, cosk, sink, rt, idn = host_consts()
    cores = list(range(8))
    x_full = np.ascontiguousarray(inp["L0_attn_x"].astype(np.float32))
    nc = _get("fused", build_fused)
    maps = [dict(
        x=np.ascontiguousarray(x_full[c]), wqkv=inp["L0_attn_w_qkv"], wproj=inp["L0_attn_w_proj"],
        cosq=cosq, sinq=sinq, cosk=cosk, sink=sink, rt=rt, idn=idn,
        f_rw=inp["L0_ffn_router_w"], f_rb=inp["L0_ffn_router_b"], f_lw=inp["L0_ffn_lang_w"],
        f_lb=inp["L0_ffn_lang_b"], f_w1=inp["L0_ffn_w1"], f_b1=inp["L0_ffn_b1"],
        f_w2=inp["L0_ffn_w2"], f_b2=inp["L0_ffn_b2"],
        m_nw=inp["L1_attn_norm_w"], m_nb=inp["L1_attn_norm_b"], m_ipw=inp["L1_attn_in_proj_w"],
        m_ipb=inp["L1_attn_in_proj_b"], m_cw=inp["L1_attn_conv_w"], m_cb=inp["L1_attn_conv_b"],
        m_xpw=inp["L1_attn_x_proj_w"], m_xpb=inp["L1_attn_x_proj_b"], m_dpw=inp["L1_attn_dt_proj_w"],
        m_dpb=inp["L1_attn_dt_proj_b"], m_alog=inp["L1_attn_A_log"], m_dpar=inp["L1_attn_D_param"],
        m_opw=inp["L1_attn_out_proj_w"], m_opb=inp["L1_attn_out_proj_b"],
        g_rW=inp["L1_ffn_router_W"], g_rb=inp["L1_ffn_router_b"], g_sW1=inp["L1_ffn_sW1"],
        g_sb1=inp["L1_ffn_sb1"], g_sW2=inp["L1_ffn_sW2"], g_sb2=inp["L1_ffn_sb2"],
        g_eW1=inp["L1_ffn_eW1"], g_eb1=inp["L1_ffn_eb1"], g_eW2=inp["L1_ffn_eW2"],
        g_eb2=inp["L1_ffn_eb2"]) for c in cores]
    r = run_bass_kernel_spmd(nc, maps, core_ids=cores)
    outs = [r.results[c]["out"] for c in cores]
    return np.stack(outs, axis=0).astype(np.float32)
